# version 09
# baseline (speedup 1.0000x reference)
import numpy as np
from contextlib import ExitStack

import concourse.bass as bass
import concourse.mybir as mybir
from concourse.bass_utils import run_bass_kernel_spmd

F32 = mybir.dt.float32
BF16 = mybir.dt.bfloat16
AF = mybir.ActivationFunctionType
ALU = mybir.AluOpType

NCORES = 8
T = 1024
KC = 16
DM = 2048
FC = 44
H = 16
J = 8
E = 64
PI = float(np.pi)
TWO_PI = float(2 * np.pi)
HALF_PI = float(np.pi / 2)

_DEBUG_TAPS = []
_UPTO = None


class Sched:
    ENGINES = ("sync", "gpsimd", "scalar", "vector", "tensor")

    def __init__(self, nc):
        self.nc = nc
        self.streams = {e: [] for e in self.ENGINES}
        self.sems = {}
        self.counts = {}

    def sem(self, name, stack):
        s = stack.enter_context(self.nc.semaphore(name))
        self.sems[name] = s
        self.counts[name] = 0
        return s

    def n(self, name):
        return self.counts[name]

    def op(self, eng, fn, waits=(), inc=None, amt=1):
        w = tuple((s, c) for s, c in waits if c > 0)
        self.streams[eng].append((fn, w, inc, amt))
        if inc is not None:
            self.counts[inc] += amt
            return self.counts[inc]
        return None

    def dma(self, eng, out_ap, in_ap, waits=(), inc="wq"):
        def fn(e, out_ap=out_ap, in_ap=in_ap):
            return e.dma_start(out_ap, in_ap)
        return self.op(eng, fn, waits, inc, 16)

    def realize(self, block):
        for eng in self.ENGINES:
            ops = self.streams[eng]
            if not ops:
                continue
            dec = getattr(block, eng)

            def build(e, ops=ops):
                waited = {}
                for fn, waits, inc, amt in ops:
                    for sname, cnt in waits:
                        if waited.get(sname, 0) < cnt:
                            e.wait_ge(self.sems[sname], cnt)
                            waited[sname] = cnt
                    inst = fn(e)
                    if inc is not None and inst is not None:
                        inst.then_inc(self.sems[inc], amt)
            dec(build)


def _blocks(w, kc):
    mo = w.shape[0] // 128
    a = w.reshape(mo, 128, kc, 128).transpose(0, 3, 2, 1)
    return np.ascontiguousarray(a.reshape(mo, 128, kc * 128))


def _out_cols(half):
    r = np.arange(T)
    return (r // 64) * 128 + 64 * half + (r % 64)


def _prep_host(inp):
    x = np.asarray(inp["L0_attn_hidden_states"], dtype=np.float32)

    wqkv = np.concatenate([
        _blocks(np.asarray(inp["L0_attn_Wq"], np.float32), KC),
        _blocks(np.asarray(inp["L0_attn_Wk"], np.float32), KC),
        _blocks(np.asarray(inp["L0_attn_Wv"], np.float32), KC),
    ], axis=0)
    wo = _blocks(np.asarray(inp["L0_attn_Wo"], np.float32), KC)
    wg = _blocks(np.asarray(inp["L0_ffn_wq_gate"], np.int32).astype(np.float32), KC)
    wu = _blocks(np.asarray(inp["L0_ffn_wq_up"], np.int32).astype(np.float32), KC)
    wgu = np.ascontiguousarray(
        np.stack([wg, wu], axis=1).reshape(2 * FC, 128, KC * 128))
    wd = _blocks(np.asarray(inp["L0_ffn_wq_down"], np.int32).astype(np.float32), FC)

    bqkv = np.ascontiguousarray(np.concatenate([
        np.asarray(inp["L0_attn_bq"], np.float32).reshape(H, 128).T,
        np.asarray(inp["L0_attn_bk"], np.float32).reshape(J, 128).T,
        np.asarray(inp["L0_attn_bv"], np.float32).reshape(J, 128).T,
    ], axis=1))
    bo = np.ascontiguousarray(np.asarray(inp["L0_attn_bo"], np.float32).reshape(KC, 128).T)
    sg = np.ascontiguousarray(np.asarray(inp["L0_ffn_s_gate"], np.float32).reshape(FC, 128).T)
    su = np.ascontiguousarray(np.asarray(inp["L0_ffn_s_up"], np.float32).reshape(FC, 128).T)
    sd = np.ascontiguousarray(np.asarray(inp["L0_ffn_s_down"], np.float32).reshape(KC, 128).T)
    Rt = np.ascontiguousarray(np.asarray(inp["L0_attn_rand_matrix"], np.float32).T)

    inv = (1.0 / (10000.0 ** (np.arange(0, 64, dtype=np.float64) * 2.0 / 128.0)))
    sel16 = np.zeros((16, 16 * 128), dtype=np.float32)
    for h in range(16):
        sel16[h, h * 128:(h + 1) * 128] = 1.0
    ind8 = np.zeros((128, 8 * 8), dtype=np.float32)
    for g in range(8):
        ind8[:, g * 8 + g] = 1.0
    sel8 = np.zeros((8, 8 * 128), dtype=np.float32)
    for g in range(8):
        sel8[g, g * 128:(g + 1) * 128] = 1.0

    per_core = []
    for c in range(NCORES):
        b, half = c // 2, c % 2
        xbT = np.ascontiguousarray(x[b].T)
        src = xbT[:, half * T:(half + 1) * T]
        out = xbT[:, _out_cols(half)]
        pos = half * T + np.arange(T)
        ang = pos[None, :].astype(np.float64) * inv[np.arange(128) % 64][:, None]
        cos_t = np.cos(ang).astype(np.float32)
        sin_t = np.sin(ang).astype(np.float32)
        sin_t[:64, :] *= -1.0
        per_core.append({
            "xS": np.ascontiguousarray(src), "xO": np.ascontiguousarray(out),
            "wqkv": wqkv, "wo": wo, "wgu": wgu, "wd": wd,
            "bqkv": bqkv, "bo": bo, "sg": sg, "su": su, "sd": sd,
            "Rt": Rt, "cosT": cos_t, "sinT": sin_t, "sel16": sel16,
            "ind8": ind8, "sel8": sel8,
        })
    return per_core


def build_program():
    nc = bass.Bass("TRN2", target_bir_lowering=False)
    stack = ExitStack()

    d = {}
    def din(name, shape):
        d[name] = nc.dram_tensor(name, list(shape), F32, kind="ExternalInput").ap()
    din("xS", [DM, T])
    din("xO", [DM, T])
    din("wqkv", [32, 128, KC * 128])
    din("wo", [KC, 128, KC * 128])
    din("wgu", [2 * FC, 128, KC * 128])
    din("wd", [KC, 128, FC * 128])
    din("bqkv", [128, 32])
    din("bo", [128, KC])
    din("sg", [128, FC])
    din("su", [128, FC])
    din("sd", [128, KC])
    din("Rt", [128, E])
    din("cosT", [128, T])
    din("sinT", [128, T])
    din("sel16", [16, 16 * 128])
    din("ind8", [128, 8 * 8])
    din("sel8", [8, 8 * 128])
    yT = nc.dram_tensor("yT", [DM, T], F32, kind="ExternalOutput").ap()

    S = Sched(nc)
    for s in ("wq", "xq", "st", "pe", "act", "dve", "pool"):
        S.sem(s, stack)

    def finish(allv):
        if _DEBUG_TAPS:
            fin_waits = [("pe", S.n("pe")), ("dve", S.n("dve")),
                         ("act", S.n("act")), ("pool", S.n("pool")),
                         ("xq", S.n("xq")), ("wq", S.n("wq"))]
            for name in _DEBUG_TAPS:
                if name not in allv:
                    continue
                tsb = allv[name]
                ap = nc.dram_tensor("tap_" + name, list(tsb[:].shape), F32,
                                    kind="ExternalOutput").ap()
                S.dma("gpsimd", ap, tsb[:], waits=fin_waits, inc="st")
        S.op("sync", lambda e: e.nop(), waits=[("st", S.n("st"))])
        block = stack.enter_context(nc.Block())
        S.realize(block)
        stack.close()
        return nc

    sb_guards = {}
    def sba(name, shape, dt, side="left"):
        g = nc.sbuf_tensor(name, list(shape), dt, side=side)
        t = g.__enter__()
        sb_guards[name] = g
        return t
    def sbf(name):
        sb_guards.pop(name).__exit__(None, None, None)
    ps_guards = {}
    def psa(name, shape):
        g = nc.psum_tensor(name, list(shape), F32)
        t = g.__enter__()
        ps_guards[name] = g
        return t
    def psf(name):
        ps_guards.pop(name).__exit__(None, None, None)

    Rt_bf = sba("Rt_bf", [128, E], BF16)
    bqkv_sb = sba("bqkv_sb", [128, 32], F32)
    bo_sb = sba("bo_sb", [128, KC], F32)
    sg_sb = sba("sg_sb", [128, FC], F32)
    su_sb = sba("su_sb", [128, FC], F32)
    sd_sb = sba("sd_sb", [128, KC], F32)
    ones_c = sba("ones_c", [128, 1], BF16)
    ones_r = sba("ones_r", [1, 128], BF16)
    sinbias = sba("sinbias", [128, 1], F32)
    wpool = sba("wpool", [128, 6144], BF16)
    wv = [wpool[:, i * 2048:(i + 1) * 2048] for i in range(3)]
    wdv = [wpool[:, i * 2816:(i + 1) * 2816] for i in range(2)]
    tmpf = [sba(f"tmpf{i}", [128, T], F32) for i in range(3)]
    rows = sba("rows", [1, 3 * T], F32)
    row = [rows[:, i * T:(i + 1) * T] for i in range(3)]
    browt = sba("browt", [1, 2 * T], BF16)
    brow = [browt[:, i * T:(i + 1) * T] for i in range(2)]

    def xc(t, c):
        return t[:, c * T:(c + 1) * T]

    def MM(out, l, r, st_, sp):
        return lambda e: e.matmul(out, l, r, start=st_, stop=sp,
                                  skip_group_check=True)

    S.dma("gpsimd", Rt_bf[:], d["Rt"])
    for dst, src in ((bqkv_sb, "bqkv"), (bo_sb, "bo"), (sg_sb, "sg"),
                     (su_sb, "su"), (sd_sb, "sd")):
        S.dma("sync", dst[:], d[src], inc="xq")
    n_cxq = S.n("xq")
    S.op("vector", lambda e: e.memset(ones_c[:], 1.0), inc="dve")
    S.op("vector", lambda e: e.memset(ones_r[:], 1.0), inc="dve")
    S.op("vector", lambda e: e.memset(sinbias[0:64, :], 0.0), inc="dve")
    n_memset = S.op("vector", lambda e: e.memset(sinbias[64:128, :], -HALF_PI),
                    inc="dve")

    def emit_ln(tag, x_dram, xn_t, gate_waits):
        xbf = sba("xbf" + tag, [128, KC * T], BF16, side="right")
        tb = sba("tb" + tag, [128, 2 * T], BF16, side="right")
        sq_t = [tb[:, 0:T], tb[:, T:2 * T]]
        xv = xbf[:].rearrange("p (c t) -> p c t", c=KC)
        dv_ = x_dram.rearrange("(c p) t -> p c t", c=KC, p=128)
        n_xh = []
        for qtr in range(4):
            S.dma("gpsimd", xv[:, qtr * 4:(qtr + 1) * 4],
                  dv_[:, qtr * 4:(qtr + 1) * 4], waits=gate_waits)
            gate_waits = ()
            n_xh.append(S.n("wq"))
        ps_sx = psa("ps_sx" + tag, [1, T])
        ps_sq = psa("ps_sq" + tag, [1, T])
        sq_done = [0, 0]
        red_done = [0, 0]
        for c in range(KC):
            p = c % 2
            n_x = n_xh[c // 4]
            w_sq = [("wq", n_x)] + ([("pe", red_done[p])] if c >= 2 else [])
            sq_done[p] = S.op("scalar",
                lambda e, c=c, p=p: e.activation(sq_t[p][:], xc(xbf, c), AF.Square),
                waits=w_sq, inc="act")
            for hf in range(2):
                S.op("tensor", MM(ps_sx[:, hf * 512:(hf + 1) * 512], ones_c[:],
                                  xc(xbf, c)[:, hf * 512:(hf + 1) * 512],
                                  c == 0, c == KC - 1),
                     waits=[("wq", n_x), ("dve", n_memset)], inc="pe")
            for hf in range(2):
                red_done[p] = S.op("tensor",
                     MM(ps_sq[:, hf * 512:(hf + 1) * 512], ones_c[:],
                        sq_t[p][:, hf * 512:(hf + 1) * 512], c == 0, c == KC - 1),
                     waits=[("act", sq_done[p])], inc="pe")
        n_red = S.n("pe")
        v_m = S.op("vector", lambda e: e.tensor_scalar(row[0], ps_sx[:], 1.0 / DM,
                                                       None, ALU.mult),
                   waits=[("pe", n_red), ("xq", n_cxq)], inc="dve")
        S.op("vector", lambda e: e.tensor_tensor(row[1], row[0], row[0], ALU.mult),
             inc="dve")
        v_mbf = S.op("vector", lambda e: e.tensor_copy(brow[1][:], row[0]),
                     inc="dve")
        v_var = S.op("vector", lambda e: e.scalar_tensor_tensor(
                        row[2], ps_sq[:], 1.0 / DM, row[1], ALU.mult, ALU.subtract),
                     inc="dve")
        v_eps = S.op("vector", lambda e: e.tensor_scalar(row[1], row[2], 1e-5,
                                                         None, ALU.add), inc="dve")
        a_ln = S.op("scalar", lambda e: e.activation(row[2], row[1], AF.Ln),
                    waits=[("dve", v_eps), ("xq", n_cxq)], inc="act")
        v_hf = S.op("vector", lambda e: e.tensor_scalar(row[1], row[2], -0.5,
                                                        None, ALU.mult),
                    waits=[("act", a_ln)], inc="dve")
        a_rstd = S.op("scalar", lambda e: e.activation(brow[0][:], row[1], AF.Exp),
                      waits=[("dve", v_hf)], inc="act")
        psf("ps_sq" + tag); psf("ps_sx" + tag)
        ps_b1 = psa("ps_b1" + tag, [128, T])
        ps_b2 = psa("ps_b2" + tag, [128, T])
        for hf in range(2):
            S.op("tensor", MM(ps_b1[:, hf * 512:(hf + 1) * 512], ones_r[:],
                              brow[1][:, hf * 512:(hf + 1) * 512], True, True),
                 waits=[("dve", v_var)], inc="pe")
        for hf in range(2):
            S.op("tensor", MM(ps_b2[:, hf * 512:(hf + 1) * 512], ones_r[:],
                              brow[0][:, hf * 512:(hf + 1) * 512], True, True),
                 waits=[("act", a_rstd)], inc="pe")
        n_bc = S.n("pe")
        n_xn = 0
        for c in range(KC):
            S.op("vector", lambda e, c=c: e.tensor_tensor(
                    tmpf[c % 2][:], xc(xbf, c), ps_b1[:], ALU.subtract),
                 waits=[("pe", n_bc)], inc="dve")
            n_xn = S.op("vector", lambda e, c=c: e.tensor_tensor(
                    xc(xn_t, c), tmpf[c % 2][:], ps_b2[:], ALU.mult), inc="dve")
        psf("ps_b2" + tag); psf("ps_b1" + tag)
        sbf("tb" + tag); sbf("xbf" + tag)
        return n_xn

    xn_s = sba("xn_s", [128, KC * T], BF16, side="right")
    n_xn = emit_ln("S", d["xS"], xn_s, [])
    if _UPTO == "LN1":
        allv = dict(locals()); allv["xn"] = xn_s
        return finish(allv)

    cos_bf = sba("cos_bf", [128, T], BF16, side="right")
    sin_bf = sba("sin_bf", [128, T], BF16, side="right")
    tmpb3 = sba("tmpb3", [128, 3 * T], BF16, side="right")
    tmpc = [tmpb3[:, i * T:(i + 1) * T] for i in range(3)]
    sqt2 = sba("sqt2", [128, 2 * T], BF16, side="right")
    sqt = [sqt2[:, 0:T], sqt2[:, T:2 * T]]
    c_a1 = sba("c_a1", [128, T], BF16, side="right")
    c_a2 = sba("c_a2", [128, T], BF16, side="right")
    rn8_bf = sba("rn8_bf", [8, T], BF16, side="right")
    ind8_bf = sba("ind8_bf", [128, 64], BF16, side="right")
    sel8_bf = sba("sel8_bf", [8, 8 * 128], BF16, side="right")
    A_sb = sba("A_sb", [16, J * T], BF16, side="right")
    attn = sba("attn", [128, H * T], BF16, side="right")
    v_all = sba("v_all", [128, J * T], BF16, side="right")
    qp = sba("qp", [128, H * T], BF16, side="right")
    kp = sba("kp", [128, J * T], BF16, side="right")
    S.dma("gpsimd", cos_bf[:], d["cosT"], waits=[("dve", n_xn)])
    S.dma("gpsimd", sin_bf[:], d["sinT"])
    S.dma("gpsimd", ind8_bf[:], d["ind8"])
    S.dma("gpsimd", sel8_bf[:], d["sel8"])
    n_tables = S.n("wq")

    pp = [psa(f"pp{i}", [128, T]) for i in range(2)]
    ps_ss8 = psa("ps_ss8", [8, T])
    ps_sh = psa("ps_sh", [128, T])
    wqkv_free = [0, 0, 0]
    pp_evac = [0, 0]
    pp_free4 = [0, 0]
    first_w = [("dve", n_xn)]

    def emit_blk(bi, m, mid=None):
        sl, wb = bi % 2, bi % 3
        S.dma("gpsimd", wv[wb], d["wqkv"][m], waits=[("pe", wqkv_free[wb])])
        n_w = S.n("wq")
        waits = [("wq", n_w), ("act", pp_evac[sl]), ("dve", pp_free4[sl])] + \
            (first_w if bi == 0 else [])
        last_mm = 0
        for kc in range(KC):
            for hf in range(2):
                last_mm = S.op("tensor",
                    MM(pp[sl][:, hf * 512:(hf + 1) * 512],
                       wv[wb][:, kc * 128:(kc + 1) * 128],
                       xc(xn_s, kc)[:, hf * 512:(hf + 1) * 512],
                       kc == 0, kc == KC - 1),
                    waits=waits, inc="pe")
                waits = ()
            if kc == 7 and mid is not None:
                mid()
        wqkv_free[wb] = last_mm
        return last_mm, sl

    for bi, m in enumerate(range(24, 32)):
        last_mm, sl = emit_blk(bi, m)
        pp_evac[sl] = S.op("scalar", lambda e, ppv=pp[sl][:], m=m: e.activation(
                xc(v_all, m - 24), ppv, AF.Identity,
                bias=bqkv_sb[:, m:m + 1], scale=1.0),
             waits=[("pe", last_mm)], inc="act")
    n_v_evac = max(pp_evac)

    hstate = [dict() for _ in range(24)]
    r3_prev = 0
    ssmm_of = [0] * 24
    rnexp_of = [0, 0, 0]
    rnmax_of = [0, 0, 0]
    cstate = {"evp": 0, "qs": 0, "ab": 0, "sin": 0, "y2": 0}

    def emit_A(i, mid=None):
        nonlocal r3_prev
        st = hstate[i]
        last_mm, sl = emit_blk(8 + i, i, mid=mid)
        dst = xc(qp, i) if i < 16 else xc(kp, i - 16)
        st["dst"] = dst
        ev = S.op("scalar", lambda e, ppv=pp[sl][:], i=i: e.activation(
                tmpc[0][:], ppv, AF.Identity, bias=bqkv_sb[:, i:i + 1],
                scale=1.0),
             waits=[("pe", last_mm), ("dve", r3_prev)], inc="act")
        pp_evac[sl] = ev
        r1 = S.op("vector", lambda e: e.tensor_tensor(tmpc[1][:], tmpc[0][:],
                                                      cos_bf[:], ALU.mult),
                  waits=[("act", ev), ("wq", n_tables)], inc="dve")
        pp_free4[sl] = r1
        S.op("vector", lambda e: e.tensor_copy(tmpc[2][0:64, :],
                                               tmpc[0][64:128, :]), inc="dve")
        S.op("vector", lambda e: e.tensor_copy(tmpc[2][64:128, :],
                                               tmpc[0][0:64, :]), inc="dve")
        r3 = S.op("vector", lambda e: e.tensor_tensor(tmpc[0][:], tmpc[2][:],
                                                      sin_bf[:], ALU.mult),
                  inc="dve")
        r3_prev = r3
        S.op("vector", lambda e, dst=dst: e.tensor_tensor(
                dst, tmpc[1][:], tmpc[0][:], ALU.add), inc="dve")
        st["sqm"] = S.op("vector", lambda e, dst=dst, i=i: e.tensor_tensor(
                sqt[i % 2][:], dst, dst, ALU.mult),
             waits=[("pe", ssmm_of[i - 2] if i >= 2 else 0)], inc="dve")

    def emit_ss(i):
        g, idx = i // 8, i % 8
        w = [("dve", hstate[i]["sqm"])]
        if idx == 0 and g >= 1:
            w.append(("dve", rnmax_of[g - 1]))
        last = 0
        for hf in range(2):
            last = S.op("tensor", MM(ps_ss8[:, hf * 512:(hf + 1) * 512],
                                     ind8_bf[:, idx * 8:(idx + 1) * 8],
                                     sqt[i % 2][:, hf * 512:(hf + 1) * 512],
                                     idx == 0, idx == 7),
                 waits=w, inc="pe")
            w = ()
        ssmm_of[i] = last

    def emit_B(g):
        rnmax_of[g] = S.op("vector", lambda e: e.tensor_scalar(
                tmpf[0][0:8, :], ps_ss8[:], 1e-12, None, ALU.max),
             waits=[("pe", ssmm_of[g * 8 + 7])], inc="dve")
        lnb = S.op("scalar", lambda e: e.activation(tmpf[1][0:8, :],
                                                    tmpf[0][0:8, :], AF.Ln),
                   waits=[("dve", rnmax_of[g])], inc="act")
        sc = S.op("vector", lambda e: e.tensor_scalar(
                tmpf[0][0:8, :], tmpf[1][0:8, :], -0.5, None, ALU.mult),
             waits=[("act", lnb)], inc="dve")
        rnexp_of[g] = S.op("scalar", lambda e: e.activation(
                rn8_bf[:], tmpf[0][0:8, :], AF.Exp),
             waits=[("dve", sc)], inc="act")

    tstate = {}

    def emit_C_bc(i, tile):
        g, idx = i // 8, i % 8
        ts_ = tstate.setdefault(id(tile), {"evp": 0, "qs": 0})
        w = [("act", rnexp_of[g]), ("act", ts_["evp"]), ("dve", ts_["qs"])]
        bc = 0
        for hf in range(2):
            bc = S.op("tensor", MM(tile[:, hf * 512:(hf + 1) * 512],
                                   sel8_bf[:, idx * 128:(idx + 1) * 128],
                                   rn8_bf[:, hf * 512:(hf + 1) * 512],
                                   True, True),
                 waits=w, inc="pe")
            w = ()
        hstate[i]["bc"] = bc

    def emit_C_rest(i, tile):
        dst = hstate[i]["dst"]
        ts_ = tstate.setdefault(id(tile), {"evp": 0, "qs": 0})
        qs = S.op("vector", lambda e, dst=dst, tile=tile: e.tensor_tensor(
                c_a2[:], dst, tile[:], ALU.mult),
             waits=[("pe", hstate[i]["bc"]), ("act", cstate["sin"])], inc="dve")
        ts_["qs"] = qs
        rp = 0
        w = [("dve", qs)]
        for hf in range(2):
            rp = S.op("tensor", MM(tile[0:64, hf * 512:(hf + 1) * 512], Rt_bf[:],
                                   c_a2[:, hf * 512:(hf + 1) * 512], True, True),
                 waits=w, inc="pe")
            w = ()
        evp = S.op("scalar", lambda e, tile=tile: e.activation(c_a2[0:64, :],
                                                    tile[0:64, :], AF.Copy),
                   waits=[("pe", rp), ("dve", cstate["ab"])], inc="act")
        ts_["evp"] = evp
        S.op("vector", lambda e: e.tensor_scalar(
                c_a1[0:64, :], c_a2[0:64, :], PI, TWO_PI,
                ALU.is_gt, ALU.mult), waits=[("act", evp)], inc="dve")
        S.op("vector", lambda e: e.tensor_scalar(
                c_a1[64:128, :], c_a2[0:64, :], -PI, TWO_PI,
                ALU.is_lt, ALU.mult), inc="dve")
        S.op("vector", lambda e: e.scalar_tensor_tensor(
                c_a2[64:128, :], c_a1[0:64, :], -1.0, c_a2[0:64, :],
                ALU.mult, ALU.add), inc="dve")
        S.op("vector", lambda e: e.tensor_tensor(
                c_a2[0:64, :], c_a2[64:128, :], c_a1[64:128, :], ALU.add),
             inc="dve")
        ab = S.op("vector", lambda e: e.scalar_tensor_tensor(
                c_a2[64:128, :], c_a2[0:64, :], -1.0, c_a2[0:64, :],
                ALU.mult, ALU.max), inc="dve")
        cstate["ab"] = ab
        cstate["sin"] = S.op("scalar", lambda e, dst=dst: e.activation(
                dst, c_a2[:], AF.Sin, bias=sinbias[:, 0:1], scale=1.0),
                waits=[("dve", ab)], inc="act")

    for g in range(3):
        for idx in range(8):
            i = g * 8 + idx
            mid = None
            if g >= 1:
                mid = (lambda pi=(g - 1) * 8 + idx: emit_C_bc(pi, ps_sh))
            emit_A(i, mid=mid)
            if idx >= 1:
                emit_ss(i - 1)
            if g >= 1:
                emit_C_rest((g - 1) * 8 + idx, ps_sh)
        emit_ss(g * 8 + 7)
        emit_B(g)
    psf("ps_sh"); psf("ps_ss8")
    sh2 = [psa("sh2a", [128, T]), psa("sh2b", [128, T])]
    emit_C_bc(16, sh2[0])
    for idx in range(8):
        if idx < 7:
            emit_C_bc(17 + idx, sh2[(idx + 1) % 2])
        emit_C_rest(16 + idx, sh2[idx % 2])

    n_feats = cstate["sin"]
    psf("sh2b"); psf("sh2a")
    psf("pp1"); psf("pp0")
    pp.clear()
    if _UPTO == "QKV":
        allv = dict(locals()); allv["xn"] = xn_s
        return finish(allv)

    ps_A = [psa(f"ps_A{i}", [16, 512]) for i in range(2)]
    ps_sxO = psa("ps_sxO", [1, T])
    ps_sqO = psa("ps_sqO", [1, T])
    xO_v = d["xO"].rearrange("(c p) t -> c p t", c=KC, p=128)
    ln2 = {"sq": [0, 0], "red": [0, 0], "sx": [0, 0], "subs": [0, 0]}

    def emit_ln2_stats(c):
        p = c % 2
        w = [("act", ln2["sq"][p]), ("pe", ln2["sx"][p])]
        if c == 0:
            w += [("dve", hstate[23]["sqm"]), ("pe", ssmm_of[23])]
        S.dma("gpsimd", tmpc[p][:], xO_v[c], waits=w)
        nw = S.n("wq")
        w2 = [("wq", nw)]
        if c <= 1:
            w2.append(("pe", ssmm_of[23]))
        else:
            w2.append(("pe", ln2["red"][p]))
        ln2["sq"][p] = S.op("scalar", lambda e, p=p: e.activation(
                sqt[p][:], tmpc[p][:], AF.Square), waits=w2, inc="act")
        for hf in range(2):
            ln2["sx"][p] = S.op("tensor",
                 MM(ps_sxO[:, hf * 512:(hf + 1) * 512], ones_c[:],
                    tmpc[p][:, hf * 512:(hf + 1) * 512], c == 0, c == KC - 1),
                 waits=[("wq", nw)], inc="pe")
        for hf in range(2):
            ln2["red"][p] = S.op("tensor",
                 MM(ps_sqO[:, hf * 512:(hf + 1) * 512], ones_c[:],
                    sqt[p][:, hf * 512:(hf + 1) * 512], c == 0, c == KC - 1),
                 waits=[("act", ln2["sq"][p])], inc="pe")

    qp_v = qp[:].rearrange("p (h t) -> p t h", h=H, t=T)
    kp_v = kp[:].rearrange("p (j t) -> p t j", j=J, t=T)
    A_v = A_sb[:].rearrange("p (j b t) -> p b t j", j=J, b=T // 64, t=64)
    aevac = [0, 0]
    for blk in range(T // 64):
        a = blk % 2
        waits = [("act", n_feats), ("dve", aevac[a])]
        last_mm = 0
        for t64 in range(64):
            t = blk * 64 + t64
            last_mm = S.op("tensor", lambda e, a=a, t=t, t64=t64: e.matmul(
                    ps_A[a][:, t64 * 8:(t64 + 1) * 8], qp_v[:, t, :],
                    kp_v[:, t, :], start=True, stop=True,
                    skip_group_check=True),
                 waits=waits, inc="pe")
            waits = ()
        aevac[a] = S.op("vector", lambda e, a=a, blk=blk: e.tensor_scalar(
                A_v[:, blk], ps_A[a][:].rearrange("p (t j) -> p t j", t=64, j=J),
                2.0 / 128.0, None, ALU.mult),
             waits=[("pe", last_mm)], inc="dve")
        emit_ln2_stats(blk)
    n_A = max(aevac)
    v_m = S.op("vector", lambda e: e.tensor_scalar(row[0], ps_sxO[:], 1.0 / DM,
                                                   None, ALU.mult),
               waits=[("pe", S.n("pe"))], inc="dve")
    S.op("vector", lambda e: e.tensor_tensor(row[1], row[0], row[0], ALU.mult),
         inc="dve")
    v_mbf = S.op("vector", lambda e: e.tensor_copy(brow[1][:], row[0]), inc="dve")
    v_var = S.op("vector", lambda e: e.scalar_tensor_tensor(
                    row[2], ps_sqO[:], 1.0 / DM, row[1], ALU.mult, ALU.subtract),
                 inc="dve")
    v_eps = S.op("vector", lambda e: e.tensor_scalar(row[1], row[2], 1e-5,
                                                     None, ALU.add), inc="dve")
    a_ln = S.op("scalar", lambda e: e.activation(row[2], row[1], AF.Ln),
                waits=[("dve", v_eps)], inc="act")
    v_hf = S.op("vector", lambda e: e.tensor_scalar(row[1], row[2], -0.5,
                                                    None, ALU.mult),
                waits=[("act", a_ln)], inc="dve")
    a_rstd = S.op("scalar", lambda e: e.activation(brow[0][:], row[1], AF.Exp),
                  waits=[("dve", v_hf)], inc="act")
    if _UPTO == "ST1":
        return finish(dict(locals()))
    psf("ps_sqO"); psf("ps_sxO")
    psf("ps_A1"); psf("ps_A0")
    sbf("kp"); sbf("qp")
    ps_b1O = psa("ps_b1O", [128, T])
    ps_b2O = psa("ps_b2O", [128, T])
    for hf in range(2):
        S.op("tensor", MM(ps_b1O[:, hf * 512:(hf + 1) * 512], ones_r[:],
                          brow[1][:, hf * 512:(hf + 1) * 512], True, True),
             waits=[("dve", v_var)], inc="pe")
    n_bcO = 0
    for hf in range(2):
        n_bcO = S.op("tensor", MM(ps_b2O[:, hf * 512:(hf + 1) * 512], ones_r[:],
                                  brow[0][:, hf * 512:(hf + 1) * 512], True, True),
             waits=[("act", a_rstd)], inc="pe")
    n_apply = 0

    def emit_ln2_apply(c):
        nonlocal n_apply
        p = c % 2
        w = [("dve", ln2["subs"][p])]
        if c == 0:
            w += [("act", ln2["sq"][1]), ("pe", ln2["red"][1])]
        S.dma("gpsimd", tmpc[p][:], xO_v[c], waits=w)
        nw = S.n("wq")
        ln2["subs"][p] = S.op("vector", lambda e, p=p: e.tensor_tensor(
                tmpc[2][:], tmpc[p][:], ps_b1O[:], ALU.subtract),
             waits=[("wq", nw), ("pe", n_bcO)], inc="dve")
        n_apply = S.op("vector", lambda e, c=c: e.tensor_tensor(
                xc(xn_s, c), tmpc[2][:], ps_b2O[:], ALU.mult), inc="dve")

    sel_bf = sba("sel_bf", [16, 16 * 128], BF16, side="right")
    S.dma("gpsimd", sel_bf[:], d["sel16"],
          waits=[("pe", S.n("pe")), ("dve", n_A)])
    n_selw = S.n("wq")
    t8s = sba("t8s", [128, 4 * T], BF16, side="right")
    t8 = [t8s[:, i * T:(i + 1) * T] for i in range(4)]
    t4 = [tmpf[0], tmpf[1], tmpf[2]]
    ps_abc = [psa(f"ps_abc{i}", [128, T]) for i in range(2)]
    abc_free = [n_A, n_A]
    prev_fin = 0
    prev_fin_pool = 0
    for h in range(H):
        def bmult(j, slot, extra_dve):
            a = j % 2
            bc = 0
            for hf in range(2):
                bc = S.op("tensor", lambda e, a=a, h=h, j=j, hf=hf: e.matmul(
                        ps_abc[a][:, hf * 512:(hf + 1) * 512],
                        sel_bf[:, h * 128:(h + 1) * 128],
                        A_sb[:, j * T + hf * 512:j * T + (hf + 1) * 512],
                        start=True, stop=True, skip_group_check=True),
                     waits=[("dve", abc_free[a]), ("act", n_v_evac),
                            ("wq", n_selw)], inc="pe")
            md = S.op("vector", lambda e, a=a, j=j, slot=slot: e.tensor_tensor(
                    t8[slot][:], xc(v_all, j), ps_abc[a][:], ALU.mult),
                 waits=[("pe", bc), ("dve", extra_dve),
                        ("pool", prev_fin_pool)], inc="dve")
            abc_free[a] = md
            return md
        m0 = bmult(0, 0, prev_fin)
        m1 = bmult(1, 1, prev_fin)
        s01 = S.op("vector", lambda e: e.tensor_tensor(t4[0][:], t8[0][:],
                                                       t8[1][:], ALU.add),
                   waits=[("dve", m1)], inc="dve")
        m2 = bmult(2, 2, prev_fin)
        m3 = bmult(3, 3, prev_fin)
        s23 = S.op("gpsimd", lambda e: e.tensor_tensor(t4[1][:], t8[2][:],
                                                       t8[3][:], ALU.add),
                   waits=[("dve", m3)], inc="pool")
        m4 = bmult(4, 0, s01)
        m5 = bmult(5, 1, s01)
        s03 = S.op("vector", lambda e: e.tensor_tensor(t4[0][:], t4[0][:],
                                                       t4[1][:], ALU.add),
                   waits=[("pool", s23)], inc="dve")
        s45 = S.op("vector", lambda e: e.tensor_tensor(t4[1][:], t8[0][:],
                                                       t8[1][:], ALU.add),
                   waits=[("dve", m5)], inc="dve")
        m6 = bmult(6, 2, s03)
        m7 = bmult(7, 3, s03)
        s67 = S.op("gpsimd", lambda e: e.tensor_tensor(t4[2][:], t8[2][:],
                                                       t8[3][:], ALU.add),
                   waits=[("dve", m7)], inc="pool")
        s47 = S.op("gpsimd", lambda e: e.tensor_tensor(t4[1][:], t4[1][:],
                                                       t4[2][:], ALU.add),
                   waits=[("dve", s45)], inc="pool")
        prev_fin = S.op("vector", lambda e, h=h: e.tensor_tensor(
                xc(attn, h), t4[0][:], t4[1][:], ALU.add),
             waits=[("dve", s03), ("pool", s47)], inc="dve")
        prev_fin_pool = s47
        emit_ln2_apply(h)
    n_attn = prev_fin
    psf("ps_abc1"); psf("ps_abc0")
    sbf("t8s"); sbf("sel_bf"); sbf("v_all")
    if _UPTO == "ST2":
        return finish(dict(locals()))

    corr_sb = sba("corr_sb", [128, T], BF16)
    corrh_sb = sba("corrh_sb", [128, T], BF16)
    ps_S = psa("ps_S", [1, T])
    for c in range(KC):
        for hf in range(2):
            S.op("tensor", MM(ps_S[:, hf * 512:(hf + 1) * 512], ones_c[:],
                              xc(xn_s, c)[:, hf * 512:(hf + 1) * 512],
                              c == 0, c == KC - 1),
                 waits=[("dve", n_apply)], inc="pe")
    n_S = S.n("pe")
    v_s128 = S.op("vector", lambda e: e.tensor_scalar(brow[1][:], ps_S[:],
                                                      128.0, None, ALU.mult),
                  waits=[("pe", n_S)], inc="dve")
    ps_bcC = psa("ps_bcC", [128, T])
    for hf in range(2):
        S.op("tensor", MM(ps_bcC[:, hf * 512:(hf + 1) * 512], ones_r[:],
                          brow[1][:, hf * 512:(hf + 1) * 512], True, True),
             waits=[("dve", v_s128)], inc="pe")
    n_cb = S.n("pe")
    a_corr = S.op("scalar", lambda e: e.activation(corr_sb[:], ps_bcC[:], AF.Copy),
                  waits=[("pe", n_cb)], inc="act")
    psf("ps_bcC"); psf("ps_S")
    psf("ps_b2O"); psf("ps_b1O")
    pp.append(psa("ppA", [128, T]))
    pp.append(psa("ppB", [128, T]))

    o_sb = sba("o_sb", [128, KC * T], BF16)
    attn_v = attn[:].rearrange("p (h w k) -> p h k w", h=H, w=64, k=16)
    wo_free = [0, 0, 0]
    o_evac = [0, 0]
    for m in range(KC):
        sl = m % 2
        wb = m % 3
        S.dma("gpsimd", wv[wb], d["wo"][m], waits=[("pe", wo_free[wb])])
        n_w = S.n("wq")
        waits = [("wq", n_w), ("act", o_evac[sl]), ("dve", n_attn)]
        last_mm = 0
        for kc in range(KC):
            for nh in range(2):
                last_mm = S.op("tensor",
                    MM(pp[sl][:, nh * 512:(nh + 1) * 512],
                       wv[wb][:, kc * 128:(kc + 1) * 128],
                       attn_v[:, nh * 8:(nh + 1) * 8, kc, :],
                       kc == 0, kc == KC - 1),
                    waits=waits, inc="pe")
                waits = ()
        wo_free[wb] = last_mm
        o_evac[sl] = S.op("scalar", lambda e, ppv=pp[sl][:], m=m: e.activation(
                xc(o_sb, m), ppv, AF.Identity, bias=bo_sb[:, m:m + 1],
                scale=1.0),
             waits=[("pe", last_mm)], inc="act")
    n_oall = max(o_evac)
    sbf("attn"); sbf("A_sb"); sbf("sel8_bf"); sbf("ind8_bf")
    sbf("rn8_bf"); sbf("c_a2"); sbf("c_a1"); sbf("sqt2")
    sbf("tmpb3"); sbf("sin_bf"); sbf("cos_bf")
    if _UPTO == "OPRJ":
        return finish(dict(locals()))

    h_sb = sba("h_sb", [128, FC * T], BF16, side="right")
    pp.append(psa("pp2", [128, T]))
    ps_Sh = psa("ps_Sh", [1, T])
    pp_free = [0, 0, 0]
    w_free = [wo_free[0], wo_free[1], wo_free[2]]
    silu_done = 0
    h_done = 0
    last_mm = 0
    sh_pend = []
    for b in range(2 * FC):
        c, is_up = b // 2, b % 2
        sl = b % 3
        wb = b % 3
        S.dma("gpsimd", wv[wb], d["wgu"][b], waits=[("pe", w_free[wb])])
        n_w = S.n("wq")
        waits = [("wq", n_w), ("dve", pp_free[sl]), ("dve", n_apply),
                 ("act", a_corr), ("act", o_evac[sl] if sl < 2 else 0)]
        for kc in range(KC):
            for hf in range(2):
                last_mm = S.op("tensor",
                    MM(pp[sl][:, hf * 512:(hf + 1) * 512],
                       wv[wb][:, kc * 128:(kc + 1) * 128],
                       xc(xn_s, kc)[:, hf * 512:(hf + 1) * 512],
                       kc == 0, kc == KC - 1),
                    waits=waits, inc="pe")
                waits = ()
        w_free[wb] = last_mm
        if not is_up:
            g_sub = S.op("vector", lambda e, ppv=pp[sl][:]: e.tensor_tensor(
                    tmpf[0][:], ppv, corr_sb[:], ALU.subtract),
                 waits=[("pe", last_mm), ("act", silu_done)], inc="dve")
            pp_free[sl] = g_sub
            silu_done = S.op("scalar", lambda e, c=c: e.activation(
                    tmpf[1][:], tmpf[0][:], AF.Silu, bias=0.0,
                    scale=sg_sb[:, c:c + 1]),
                 waits=[("dve", g_sub)], inc="act")
        else:
            u_sub = S.op("vector", lambda e, ppv=pp[sl][:]: e.tensor_tensor(
                    tmpf[2][:], ppv, corr_sb[:], ALU.subtract),
                 waits=[("pe", last_mm)], inc="dve")
            pp_free[sl] = u_sub
            h_done = S.op("vector", lambda e, c=c: e.scalar_tensor_tensor(
                    xc(h_sb, c), tmpf[2][:], su_sb[:, c:c + 1], tmpf[1][:],
                    ALU.mult, ALU.mult),
                 waits=[("act", silu_done)], inc="dve")
            sh_pend.append((c, h_done))
        while len(sh_pend) > (1 if b < 2 * FC - 1 else 0):
            cc, hd = sh_pend.pop(0)
            for hf in range(2):
                S.op("tensor", MM(ps_Sh[:, hf * 512:(hf + 1) * 512], ones_c[:],
                                  xc(h_sb, cc)[:, hf * 512:(hf + 1) * 512],
                                  cc == 0, cc == FC - 1),
                     waits=[("dve", hd)], inc="pe")
    n_Sh = S.n("pe")
    n_hall = h_done
    v_sh = S.op("vector", lambda e: e.tensor_scalar(brow[1][:], ps_Sh[:],
                                                    128.0, None, ALU.mult),
                waits=[("pe", n_Sh)], inc="dve")
    for hf in range(2):
        S.op("tensor", MM(pp[0][:, hf * 512:(hf + 1) * 512], ones_r[:],
                          brow[1][:, hf * 512:(hf + 1) * 512], True, True),
             waits=[("dve", v_sh), ("dve", pp_free[0])], inc="pe")
    n_chbc = S.n("pe")
    a_corrh = S.op("scalar", lambda e, ppv=pp[0][:]: e.activation(corrh_sb[:], ppv, AF.Copy),
                   waits=[("pe", n_chbc)], inc="act")
    psf("ps_Sh")
    if _UPTO == "GUP":
        return finish(dict(locals()))

    gtmp = sba("gtmp", [128, T], BF16, side="right")
    xf = [sba(f"xf{i}", [128, T], F32, side="right") for i in range(2)]
    yT_v = yT.rearrange("(c p) t -> c p t", c=KC, p=128)
    xO_v = d["xO"].rearrange("(c p) t -> c p t", c=KC, p=128)
    wd_free = [S.n("pe"), S.n("pe")]
    g_done = 0
    pp_free3 = [S.n("dve")] * 3
    xf_free = [n_hall, n_hall]
    out_free = [0, 0]
    last_r = 0
    for m in range(KC):
        sl = m % 3
        S.dma("gpsimd", wdv[0], d["wd"][m][:, 0:2816],
              waits=[("pe", wd_free[0])])
        n_w0 = S.n("wq")
        S.dma("gpsimd", wdv[1], d["wd"][m][:, 2816:5632],
              waits=[("pe", wd_free[1])])
        n_w1 = S.n("wq")
        S.dma("sync", xf[m % 2][:], xO_v[m], waits=[("dve", xf_free[m % 2])],
              inc="xq")
        n_x = S.n("xq")
        waits = [("wq", n_w0), ("dve", pp_free3[sl]), ("dve", n_hall),
                 ("act", a_corrh)]
        last_mm = 0
        for kc in range(FC):
            if kc == 22:
                wd_free[0] = last_mm
                waits = [("wq", n_w1)]
            wsrc = (wdv[0][:, kc * 128:(kc + 1) * 128] if kc < 22 else
                    wdv[1][:, (kc - 22) * 128:(kc - 21) * 128])
            for hf in range(2):
                last_mm = S.op("tensor",
                    MM(pp[sl][:, hf * 512:(hf + 1) * 512], wsrc,
                       xc(h_sb, kc)[:, hf * 512:(hf + 1) * 512],
                       kc == 0, kc == FC - 1),
                    waits=waits, inc="pe")
                waits = ()
        wd_free[1] = last_mm
        d_sub = S.op("vector", lambda e, ppv=pp[sl][:]: e.tensor_tensor(
                tmpf[0][:], ppv, corrh_sb[:], ALU.subtract),
             waits=[("pe", last_mm), ("act", g_done), ("dve", last_r)],
             inc="dve")
        pp_free3[sl] = d_sub
        g_done = S.op("scalar", lambda e, m=m: e.activation(
                gtmp[:], tmpf[0][:], AF.Silu, bias=0.0,
                scale=sd_sb[:, m:m + 1]),
             waits=[("dve", d_sub)], inc="act")
        og = S.op("vector", lambda e, m=m: e.tensor_tensor(
                tmpf[0][:], gtmp[:], xc(o_sb, m), ALU.mult),
             waits=[("act", g_done), ("act", n_oall)], inc="dve")
        last_r = S.op("vector", lambda e, m=m: e.tensor_tensor(
                tmpf[1 + (m % 2)][:], tmpf[0][:], xf[m % 2][:], ALU.add),
             waits=[("xq", n_x), ("st", out_free[m % 2])], inc="dve")
        xf_free[m % 2] = last_r
        out_free[m % 2] = S.op("sync", lambda e, m=m: e.dma_start(
                yT_v[m], tmpf[1 + (m % 2)][:]),
             waits=[("dve", last_r)], inc="st", amt=16)

    return finish(dict(locals()))


_NC_CACHE = {}


def kernel(**inputs):
    per_core = _prep_host(inputs)
    key = ("prog", _UPTO, tuple(_DEBUG_TAPS))
    if key not in _NC_CACHE:
        _NC_CACHE.clear()
        _NC_CACHE[key] = build_program()
    nc = _NC_CACHE[key]
    res = run_bass_kernel_spmd(nc, per_core, core_ids=list(range(NCORES)))
    outs = [np.asarray(r["yT"], dtype=np.float32) for r in res.results]
    y = np.zeros((4, 2048, DM), dtype=np.float32)
    for c in range(NCORES):
        b, half = c // 2, c % 2
        y[b, _out_cols(half), :] = outs[c].T
    if _DEBUG_TAPS:
        kernel._taps = [{k: np.asarray(v) for k, v in r.items()
                         if k.startswith("tap_")} for r in res.results]
    return y


# version 16
# speedup vs baseline: 1.2496x; 1.2496x over previous
import numpy as np
from contextlib import ExitStack

import concourse.bass as bass
import concourse.mybir as mybir
from concourse.bass_utils import run_bass_kernel_spmd

F32 = mybir.dt.float32
BF16 = mybir.dt.bfloat16
AF = mybir.ActivationFunctionType
ALU = mybir.AluOpType

NCORES = 8
T = 1024
KC = 16
DM = 2048
FC = 44
H = 16
J = 8
E = 64
PI = float(np.pi)
TWO_PI = float(2 * np.pi)
HALF_PI = float(np.pi / 2)

_DEBUG_TAPS = []
_UPTO = None


class Sched:
    ENGINES = ("sync", "gpsimd", "scalar", "vector", "tensor")

    def __init__(self, nc):
        self.nc = nc
        self.streams = {e: [] for e in self.ENGINES}
        self.sems = {}
        self.counts = {}

    def sem(self, name, stack):
        s = stack.enter_context(self.nc.semaphore(name))
        self.sems[name] = s
        self.counts[name] = 0
        return s

    def n(self, name):
        return self.counts[name]

    def op(self, eng, fn, waits=(), inc=None, amt=1):
        w = tuple((s, c) for s, c in waits if c > 0)
        self.streams[eng].append((fn, w, inc, amt))
        if inc is not None:
            self.counts[inc] += amt
            return self.counts[inc]
        return None

    def dma(self, eng, out_ap, in_ap, waits=(), inc="wq"):
        def fn(e, out_ap=out_ap, in_ap=in_ap):
            return e.dma_start(out_ap, in_ap)
        return self.op(eng, fn, waits, inc, 16)

    def realize(self, block):
        for eng in self.ENGINES:
            ops = self.streams[eng]
            if not ops:
                continue
            dec = getattr(block, eng)

            def build(e, ops=ops):
                waited = {}
                for fn, waits, inc, amt in ops:
                    for sname, cnt in waits:
                        if waited.get(sname, 0) < cnt:
                            e.wait_ge(self.sems[sname], cnt)
                            waited[sname] = cnt
                    inst = fn(e)
                    if inc is not None and inst is not None:
                        inst.then_inc(self.sems[inc], amt)
            dec(build)


def _blocks(w, kc):
    mo = w.shape[0] // 128
    a = w.reshape(mo, 128, kc, 128).transpose(0, 3, 2, 1)
    return np.ascontiguousarray(a.reshape(mo, 128, kc * 128))


def _out_cols(half):
    r = np.arange(T)
    return (r // 64) * 128 + 64 * half + (r % 64)


def _prep_host(inp):
    x = np.asarray(inp["L0_attn_hidden_states"], dtype=np.float32)

    wqkv = np.concatenate([
        _blocks(np.asarray(inp["L0_attn_Wq"], np.float32), KC),
        _blocks(np.asarray(inp["L0_attn_Wk"], np.float32), KC),
        _blocks(np.asarray(inp["L0_attn_Wv"], np.float32), KC),
    ], axis=0)
    wo = _blocks(np.asarray(inp["L0_attn_Wo"], np.float32), KC)
    wg = _blocks(np.asarray(inp["L0_ffn_wq_gate"], np.int32).astype(np.float32), KC)
    wu = _blocks(np.asarray(inp["L0_ffn_wq_up"], np.int32).astype(np.float32), KC)
    wgu = np.ascontiguousarray(
        np.stack([wg, wu], axis=1).reshape(2 * FC, 128, KC * 128))
    wd = _blocks(np.asarray(inp["L0_ffn_wq_down"], np.int32).astype(np.float32), FC)

    bqkv = np.ascontiguousarray(np.concatenate([
        np.asarray(inp["L0_attn_bq"], np.float32).reshape(H, 128).T,
        np.asarray(inp["L0_attn_bk"], np.float32).reshape(J, 128).T,
        np.asarray(inp["L0_attn_bv"], np.float32).reshape(J, 128).T,
    ], axis=1))
    bo = np.ascontiguousarray(np.asarray(inp["L0_attn_bo"], np.float32).reshape(KC, 128).T)
    sg = np.ascontiguousarray(np.asarray(inp["L0_ffn_s_gate"], np.float32).reshape(FC, 128).T)
    su = np.ascontiguousarray(np.asarray(inp["L0_ffn_s_up"], np.float32).reshape(FC, 128).T)
    sd = np.ascontiguousarray(np.asarray(inp["L0_ffn_s_down"], np.float32).reshape(KC, 128).T)
    Rt = np.ascontiguousarray(np.asarray(inp["L0_attn_rand_matrix"], np.float32).T)

    inv = (1.0 / (10000.0 ** (np.arange(0, 64, dtype=np.float64) * 2.0 / 128.0)))
    sel16 = np.zeros((16, 16 * 128), dtype=np.float32)
    for h in range(16):
        sel16[h, h * 128:(h + 1) * 128] = 1.0
    ind8 = np.zeros((128, 8 * 8), dtype=np.float32)
    for g in range(8):
        ind8[:, g * 8 + g] = 1.0
    sel8 = np.zeros((8, 8 * 128), dtype=np.float32)
    for g in range(8):
        sel8[g, g * 128:(g + 1) * 128] = 1.0

    per_core = []
    for c in range(NCORES):
        b, half = c // 2, c % 2
        xbT = np.ascontiguousarray(x[b].T)
        src = xbT[:, half * T:(half + 1) * T]
        out = xbT[:, _out_cols(half)]
        pos = half * T + np.arange(T)
        ang = pos[None, :].astype(np.float64) * inv[np.arange(128) % 64][:, None]
        cos_t = np.cos(ang).astype(np.float32)
        sin_t = np.sin(ang).astype(np.float32)
        sin_t[:64, :] *= -1.0
        per_core.append({
            "xS": np.ascontiguousarray(src), "xO": np.ascontiguousarray(out),
            "wqkv": wqkv, "wo": wo, "wgu": wgu, "wd": wd,
            "bqkv": bqkv, "bo": bo, "sg": sg, "su": su, "sd": sd,
            "Rt": Rt, "cosT": cos_t, "sinT": sin_t, "sel16": sel16,
            "ind8": ind8, "sel8": sel8,
        })
    return per_core


def build_program():
    nc = bass.Bass("TRN2", target_bir_lowering=False)
    stack = ExitStack()

    d = {}
    def din(name, shape):
        d[name] = nc.dram_tensor(name, list(shape), F32, kind="ExternalInput").ap()
    din("xS", [DM, T])
    din("xO", [DM, T])
    din("wqkv", [32, 128, KC * 128])
    din("wo", [KC, 128, KC * 128])
    din("wgu", [2 * FC, 128, KC * 128])
    din("wd", [KC, 128, FC * 128])
    din("bqkv", [128, 32])
    din("bo", [128, KC])
    din("sg", [128, FC])
    din("su", [128, FC])
    din("sd", [128, KC])
    din("Rt", [128, E])
    din("cosT", [128, T])
    din("sinT", [128, T])
    din("sel16", [16, 16 * 128])
    din("ind8", [128, 8 * 8])
    din("sel8", [8, 8 * 128])
    yT = nc.dram_tensor("yT", [DM, T], F32, kind="ExternalOutput").ap()

    S = Sched(nc)
    for s in ("wq", "xq", "st", "pe", "act", "dve", "pool"):
        S.sem(s, stack)

    def finish(allv):
        if _DEBUG_TAPS:
            fin_waits = [("pe", S.n("pe")), ("dve", S.n("dve")),
                         ("act", S.n("act")), ("pool", S.n("pool")),
                         ("xq", S.n("xq")), ("wq", S.n("wq"))]
            for name in _DEBUG_TAPS:
                if name not in allv:
                    continue
                tsb = allv[name]
                ap = nc.dram_tensor("tap_" + name, list(tsb[:].shape), F32,
                                    kind="ExternalOutput").ap()
                S.dma("gpsimd", ap, tsb[:], waits=fin_waits, inc="st")
        S.op("sync", lambda e: e.nop(), waits=[("st", S.n("st"))])
        block = stack.enter_context(nc.Block())
        S.realize(block)
        stack.close()
        return nc

    sb_guards = {}
    def sba(name, shape, dt, side="left"):
        g = nc.sbuf_tensor(name, list(shape), dt, side=side)
        t = g.__enter__()
        sb_guards[name] = g
        return t
    def sbf(name):
        sb_guards.pop(name).__exit__(None, None, None)
    ps_guards = {}
    def psa(name, shape):
        g = nc.psum_tensor(name, list(shape), F32)
        t = g.__enter__()
        ps_guards[name] = g
        return t
    def psf(name):
        ps_guards.pop(name).__exit__(None, None, None)

    Rt_bf = sba("Rt_bf", [128, E], BF16)
    bqkv_sb = sba("bqkv_sb", [128, 32], F32)
    bo_sb = sba("bo_sb", [128, KC], F32)
    sg_sb = sba("sg_sb", [128, FC], F32)
    su_sb = sba("su_sb", [128, FC], F32)
    sd_sb = sba("sd_sb", [128, KC], F32)
    ones_c = sba("ones_c", [128, 1], BF16)
    ones_r = sba("ones_r", [1, 128], BF16)
    sinbias = sba("sinbias", [128, 1], F32)
    wpool = sba("wpool", [128, 6144], BF16)
    wv = [wpool[:, i * 2048:(i + 1) * 2048] for i in range(3)]
    wdv = [wpool[:, i * 2816:(i + 1) * 2816] for i in range(2)]
    tmpf = [sba(f"tmpf{i}", [128, T], F32) for i in range(3)]
    rows = sba("rows", [1, 3 * T], F32)
    row = [rows[:, i * T:(i + 1) * T] for i in range(3)]
    browt = sba("browt", [1, 2 * T], BF16)
    brow = [browt[:, i * T:(i + 1) * T] for i in range(2)]

    def xc(t, c):
        return t[:, c * T:(c + 1) * T]

    def MM(out, l, r, st_, sp):
        return lambda e: e.matmul(out, l, r, start=st_, stop=sp,
                                  skip_group_check=True)

    S.dma("gpsimd", Rt_bf[:], d["Rt"])
    for dst, src in ((bqkv_sb, "bqkv"), (bo_sb, "bo"), (sg_sb, "sg"),
                     (su_sb, "su"), (sd_sb, "sd")):
        S.dma("sync", dst[:], d[src], inc="xq")
    n_cxq = S.n("xq")
    S.op("vector", lambda e: e.memset(ones_c[:], 1.0), inc="dve")
    S.op("vector", lambda e: e.memset(ones_r[:], 1.0), inc="dve")
    S.op("vector", lambda e: e.memset(sinbias[0:64, :], 0.0), inc="dve")
    n_memset = S.op("vector", lambda e: e.memset(sinbias[64:128, :], -HALF_PI),
                    inc="dve")

    xn_s = sba("xn_s", [128, KC * T], BF16, side="right")
    xst = [sba(f"xst{i}", [128, T], F32, side="right") for i in range(2)]
    tb4 = sba("tb4", [128, 4 * T], BF16, side="right")
    xb = [tb4[:, 0:T], tb4[:, T:2 * T]]
    sqb = [tb4[:, 2 * T:3 * T], tb4[:, 3 * T:4 * T]]
    xS_v = d["xS"].rearrange("(c p) t -> c p t", c=KC, p=128)
    ps_sx = psa("ps_sx", [1, T])
    ps_sq = psa("ps_sq", [1, T])
    l1 = {"sq": [0, 0], "red": [0, 0], "sx": [0, 0], "cast": [0, 0],
          "sub": [0, 0]}
    for c in range(KC):
        p = c % 2
        S.dma("sync", xst[p][:], xS_v[c],
              waits=[("act", l1["sq"][p]), ("dve", l1["cast"][p]),
                     ("xq", n_cxq)], inc="xq")
        nw = S.n("xq")
        l1["cast"][p] = S.op("vector", lambda e, p=p: e.tensor_copy(
                xb[p][:], xst[p][:]),
             waits=[("xq", nw), ("pe", l1["sx"][p]), ("dve", n_memset)],
             inc="dve")
        l1["sq"][p] = S.op("scalar", lambda e, p=p: e.activation(
                sqb[p][:], xst[p][:], AF.Square),
             waits=[("xq", nw), ("pe", l1["red"][p])], inc="act")
        for hf in range(2):
            l1["sx"][p] = S.op("tensor",
                 MM(ps_sx[:, hf * 512:(hf + 1) * 512], ones_c[:],
                    xb[p][:, hf * 512:(hf + 1) * 512], c == 0, c == KC - 1),
                 waits=[("dve", l1["cast"][p])], inc="pe")
        for hf in range(2):
            l1["red"][p] = S.op("tensor",
                 MM(ps_sq[:, hf * 512:(hf + 1) * 512], ones_c[:],
                    sqb[p][:, hf * 512:(hf + 1) * 512], c == 0, c == KC - 1),
                 waits=[("act", l1["sq"][p])], inc="pe")
    n_red1 = S.n("pe")
    S.op("vector", lambda e: e.tensor_scalar(row[0], ps_sx[:], 1.0 / DM,
                                             None, ALU.mult),
         waits=[("pe", n_red1)], inc="dve")
    S.op("vector", lambda e: e.tensor_tensor(row[1], row[0], row[0], ALU.mult),
         inc="dve")
    v_mbf1 = S.op("vector", lambda e: e.tensor_copy(brow[1][:], row[0]),
                  inc="dve")
    v_var1 = S.op("vector", lambda e: e.scalar_tensor_tensor(
                    row[2], ps_sq[:], 1.0 / DM, row[1], ALU.mult, ALU.subtract),
                 inc="dve")
    v_eps1 = S.op("vector", lambda e: e.tensor_scalar(row[1], row[2], 1e-5,
                                                      None, ALU.add), inc="dve")
    a_ln1 = S.op("scalar", lambda e: e.activation(row[2], row[1], AF.Ln),
                 waits=[("dve", v_eps1)], inc="act")
    v_hf1 = S.op("vector", lambda e: e.tensor_scalar(row[1], row[2], -0.5,
                                                     None, ALU.mult),
                 waits=[("act", a_ln1)], inc="dve")
    a_rstd1 = S.op("scalar", lambda e: e.activation(brow[0][:], row[1], AF.Exp),
                   waits=[("dve", v_hf1)], inc="act")
    psf("ps_sq"); psf("ps_sx")
    ps_b1 = psa("ps_b1", [128, T])
    ps_b2 = psa("ps_b2", [128, T])
    for hf in range(2):
        S.op("tensor", MM(ps_b1[:, hf * 512:(hf + 1) * 512], ones_r[:],
                          brow[1][:, hf * 512:(hf + 1) * 512], True, True),
             waits=[("dve", v_var1)], inc="pe")
    n_bc1 = 0
    for hf in range(2):
        n_bc1 = S.op("tensor", MM(ps_b2[:, hf * 512:(hf + 1) * 512], ones_r[:],
                                  brow[0][:, hf * 512:(hf + 1) * 512], True, True),
             waits=[("act", a_rstd1)], inc="pe")
    n_xn = 0
    for c in range(KC):
        p = c % 2
        S.dma("sync", xst[p][:], xS_v[c], waits=[("dve", l1["sub"][p])],
              inc="xq")
        nw = S.n("xq")
        l1["sub"][p] = S.op("vector", lambda e, p=p: e.tensor_tensor(
                tmpf[p][:], xst[p][:], ps_b1[:], ALU.subtract),
             waits=[("xq", nw), ("pe", n_bc1)], inc="dve")
        n_xn = S.op("vector", lambda e, p=p, c=c: e.tensor_tensor(
                xc(xn_s, c), tmpf[p][:], ps_b2[:], ALU.mult), inc="dve")
    psf("ps_b2"); psf("ps_b1")
    sbf("tb4")
    sbf("xst1"); sbf("xst0")
    if _UPTO == "LN1":
        allv = dict(locals()); allv["xn"] = xn_s
        return finish(allv)

    cos_bf = sba("cos_bf", [128, T], BF16, side="right")
    sin_bf = sba("sin_bf", [128, T], BF16, side="right")
    tmpb3 = sba("tmpb3", [128, 3 * T], BF16, side="right")
    tmpc = [tmpb3[:, i * T:(i + 1) * T] for i in range(3)]
    sqt2 = sba("sqt2", [128, 2 * T], BF16, side="right")
    sqt = [sqt2[:, 0:T], sqt2[:, T:2 * T]]
    c_a1 = sba("c_a1", [128, T], BF16, side="right")
    c_a2 = sba("c_a2", [128, T], BF16, side="right")
    rn8_bf = sba("rn8_bf", [8, T], BF16, side="right")
    ind8_bf = sba("ind8_bf", [128, 64], BF16, side="right")
    sel8_bf = sba("sel8_bf", [8, 8 * 128], BF16, side="right")
    A_sb = sba("A_sb", [16, J * T], BF16, side="right")
    attn = sba("attn", [128, H * T], BF16, side="right")
    v_all = sba("v_all", [128, J * T], BF16, side="right")
    qp = sba("qp", [128, H * T], BF16, side="right")
    kp = sba("kp", [128, J * T], BF16, side="right")
    S.dma("gpsimd", cos_bf[:], d["cosT"], waits=[("dve", n_xn)])
    S.dma("gpsimd", sin_bf[:], d["sinT"])
    S.dma("gpsimd", ind8_bf[:], d["ind8"])
    S.dma("gpsimd", sel8_bf[:], d["sel8"])
    n_tables = S.n("wq")

    pp = [psa(f"pp{i}", [128, T]) for i in range(2)]
    ps_ss8 = psa("ps_ss8", [8, T])
    ps_sh = psa("ps_sh", [128, T])
    wqkv_free = [0, 0, 0]
    pp_evac = [0, 0]
    pp_free4 = [0, 0]
    first_w = [("dve", n_xn)]

    def emit_blk(bi, m, mid=None):
        sl, wb = bi % 2, bi % 3
        S.dma("gpsimd", wv[wb], d["wqkv"][m], waits=[("pe", wqkv_free[wb])])
        n_w = S.n("wq")
        waits = [("wq", n_w), ("act", pp_evac[sl]), ("dve", pp_free4[sl])] + \
            (first_w if bi == 0 else [])
        last_mm = 0
        for kc in range(KC):
            for hf in range(2):
                last_mm = S.op("tensor",
                    MM(pp[sl][:, hf * 512:(hf + 1) * 512],
                       wv[wb][:, kc * 128:(kc + 1) * 128],
                       xc(xn_s, kc)[:, hf * 512:(hf + 1) * 512],
                       kc == 0, kc == KC - 1),
                    waits=waits, inc="pe")
                waits = ()
            if kc == 7 and mid is not None:
                mid()
        wqkv_free[wb] = last_mm
        return last_mm, sl

    for bi, m in enumerate(range(24, 32)):
        last_mm, sl = emit_blk(bi, m)
        pp_evac[sl] = S.op("scalar", lambda e, ppv=pp[sl][:], m=m: e.activation(
                xc(v_all, m - 24), ppv, AF.Identity,
                bias=bqkv_sb[:, m:m + 1], scale=1.0),
             waits=[("pe", last_mm)], inc="act")
    n_v_evac = max(pp_evac)

    hstate = [dict() for _ in range(24)]
    r3_prev = 0
    st0_free = [0]
    ssmm_of = [0] * 24
    rnexp_of = [0, 0, 0]
    rnmax_of = [0, 0, 0]
    cstate = {"evp": 0, "qs": 0, "ab": 0, "sin": 0, "y2": 0}

    def emit_A_blk(i, mid=None):
        st = hstate[i]
        last_mm, sl = emit_blk(8 + i, i, mid=mid)
        st["dst"] = xc(qp, i) if i < 16 else xc(kp, i - 16)
        st["last_mm"] = last_mm
        st["sl"] = sl

    def emit_A_post(i):
        nonlocal r3_prev
        st = hstate[i]
        last_mm, sl, dst = st["last_mm"], st["sl"], st["dst"]
        ev = S.op("scalar", lambda e, ppv=pp[sl][:], i=i: e.activation(
                tmpc[0][:], ppv, AF.Identity, bias=bqkv_sb[:, i:i + 1],
                scale=1.0),
             waits=[("pe", last_mm), ("dve", r3_prev)], inc="act")
        pp_evac[sl] = ev
        r1 = S.op("vector", lambda e: e.tensor_tensor(tmpc[1][:], tmpc[0][:],
                                                      cos_bf[:], ALU.mult),
                  waits=[("act", ev), ("wq", n_tables)], inc="dve")
        pp_free4[sl] = r1
        S.op("vector", lambda e: e.tensor_copy(tmpc[2][0:64, :],
                                               tmpc[0][64:128, :]), inc="dve")
        S.op("vector", lambda e: e.tensor_copy(tmpc[2][64:128, :],
                                               tmpc[0][0:64, :]), inc="dve")
        r3 = S.op("vector", lambda e: e.tensor_tensor(tmpc[0][:], tmpc[2][:],
                                                      sin_bf[:], ALU.mult),
                  inc="dve")
        r3_prev = r3
        S.op("vector", lambda e, dst=dst: e.tensor_tensor(
                dst, tmpc[1][:], tmpc[0][:], ALU.add), inc="dve")
        st["sqm"] = S.op("vector", lambda e, dst=dst, i=i: e.tensor_tensor(
                sqt[i % 2][:], dst, dst, ALU.mult),
             waits=[("pe", ssmm_of[i - 2] if i >= 2 else 0)], inc="dve")

    def emit_ss(i):
        g, idx = i // 8, i % 8
        w = [("dve", hstate[i]["sqm"])]
        if idx == 0 and g >= 1:
            w.append(("dve", rnmax_of[g - 1]))
        last = 0
        for hf in range(2):
            last = S.op("tensor", MM(ps_ss8[:, hf * 512:(hf + 1) * 512],
                                     ind8_bf[:, idx * 8:(idx + 1) * 8],
                                     sqt[i % 2][:, hf * 512:(hf + 1) * 512],
                                     idx == 0, idx == 7),
                 waits=w, inc="pe")
            w = ()
        ssmm_of[i] = last

    def emit_B(g):
        rnmax_of[g] = S.op("vector", lambda e: e.tensor_scalar(
                tmpf[0][0:8, :], ps_ss8[:], 1e-12, None, ALU.max),
             waits=[("pe", ssmm_of[g * 8 + 7])], inc="dve")
        lnb = S.op("scalar", lambda e: e.activation(tmpf[1][0:8, :],
                                                    tmpf[0][0:8, :], AF.Ln),
                   waits=[("dve", rnmax_of[g])], inc="act")
        sc = S.op("vector", lambda e: e.tensor_scalar(
                tmpf[0][0:8, :], tmpf[1][0:8, :], -0.5, None, ALU.mult),
             waits=[("act", lnb)], inc="dve")
        rnexp_of[g] = S.op("scalar", lambda e: e.activation(
                rn8_bf[:], tmpf[0][0:8, :], AF.Exp),
             waits=[("dve", sc)], inc="act")

    tstate = {}

    def emit_C_bc(i, tile):
        g, idx = i // 8, i % 8
        ts_ = tstate.setdefault(id(tile), {"evp": 0, "qs": 0})
        w = [("act", rnexp_of[g]), ("act", ts_["evp"]), ("dve", ts_["qs"])]
        bc = 0
        for hf in range(2):
            bc = S.op("tensor", MM(tile[:, hf * 512:(hf + 1) * 512],
                                   sel8_bf[:, idx * 128:(idx + 1) * 128],
                                   rn8_bf[:, hf * 512:(hf + 1) * 512],
                                   True, True),
                 waits=w, inc="pe")
            w = ()
        hstate[i]["bc"] = bc

    def emit_C_rest(i, tile):
        dst = hstate[i]["dst"]
        ts_ = tstate.setdefault(id(tile), {"evp": 0, "qs": 0})
        qs = S.op("vector", lambda e, dst=dst, tile=tile: e.tensor_tensor(
                c_a2[:], dst, tile[:], ALU.mult),
             waits=[("pe", hstate[i]["bc"]), ("act", cstate["sin"])], inc="dve")
        ts_["qs"] = qs
        rp = 0
        w = [("dve", qs)]
        for hf in range(2):
            rp = S.op("tensor", MM(tile[0:64, hf * 512:(hf + 1) * 512], Rt_bf[:],
                                   c_a2[:, hf * 512:(hf + 1) * 512], True, True),
                 waits=w, inc="pe")
            w = ()
        evp = S.op("scalar", lambda e, tile=tile: e.activation(c_a2[0:64, :],
                                                    tile[0:64, :], AF.Copy),
                   waits=[("pe", rp), ("dve", cstate["ab"])], inc="act")
        ts_["evp"] = evp
        S.op("vector", lambda e: e.tensor_scalar(
                c_a1[0:64, :], c_a2[0:64, :], PI, TWO_PI,
                ALU.is_gt, ALU.mult), waits=[("act", evp)], inc="dve")
        S.op("vector", lambda e: e.tensor_scalar(
                c_a1[64:128, :], c_a2[0:64, :], -PI, TWO_PI,
                ALU.is_lt, ALU.mult), inc="dve")
        S.op("vector", lambda e: e.scalar_tensor_tensor(
                c_a2[64:128, :], c_a1[0:64, :], -1.0, c_a2[0:64, :],
                ALU.mult, ALU.add), inc="dve")
        S.op("vector", lambda e: e.tensor_tensor(
                c_a2[0:64, :], c_a2[64:128, :], c_a1[64:128, :], ALU.add),
             inc="dve")
        ab = S.op("vector", lambda e: e.scalar_tensor_tensor(
                c_a2[64:128, :], c_a2[0:64, :], -1.0, c_a2[0:64, :],
                ALU.mult, ALU.max), inc="dve")
        cstate["ab"] = ab
        cstate["sin"] = S.op("scalar", lambda e, dst=dst: e.activation(
                dst, c_a2[:], AF.Sin, bias=sinbias[:, 0:1], scale=1.0),
                waits=[("dve", ab)], inc="act")

    for g in range(3):
        for idx in range(8):
            i = g * 8 + idx
            mid = None
            if g >= 1:
                mid = (lambda pi=(g - 1) * 8 + idx: emit_C_bc(pi, ps_sh))
            emit_A_blk(i, mid=mid)
            if g >= 1:
                emit_C_rest((g - 1) * 8 + idx, ps_sh)
            if idx >= 1:
                emit_ss(i - 1)
            emit_A_post(i)
        emit_ss(g * 8 + 7)
        emit_B(g)
    psf("ps_sh"); psf("ps_ss8")
    sh2 = [psa("sh2a", [128, T]), psa("sh2b", [128, T])]
    emit_C_bc(16, sh2[0])
    for idx in range(8):
        if idx < 7:
            emit_C_bc(17 + idx, sh2[(idx + 1) % 2])
        emit_C_rest(16 + idx, sh2[idx % 2])

    n_feats = cstate["sin"]
    psf("sh2b"); psf("sh2a")
    psf("pp1"); psf("pp0")
    pp.clear()
    if _UPTO == "QKV":
        allv = dict(locals()); allv["xn"] = xn_s
        return finish(allv)

    ps_A = [psa(f"ps_A{i}", [16, 512]) for i in range(2)]
    ps_sxO = psa("ps_sxO", [1, T])
    ps_sqO = psa("ps_sqO", [1, T])
    xO_v = d["xO"].rearrange("(c p) t -> c p t", c=KC, p=128)
    ln2 = {"sq": [0, 0], "red": [0, 0], "sx": [0, 0], "subs": [0, 0]}

    def emit_ln2_stats(c):
        p = c % 2
        w = [("act", ln2["sq"][p]), ("pe", ln2["sx"][p])]
        if c == 0:
            w += [("dve", hstate[23]["sqm"]), ("pe", ssmm_of[23])]
        S.dma("gpsimd", tmpc[p][:], xO_v[c], waits=w)
        nw = S.n("wq")
        w2 = [("wq", nw)]
        if c <= 1:
            w2.append(("pe", ssmm_of[23]))
        else:
            w2.append(("pe", ln2["red"][p]))
        ln2["sq"][p] = S.op("scalar", lambda e, p=p: e.activation(
                sqt[p][:], tmpc[p][:], AF.Square), waits=w2, inc="act")
        for hf in range(2):
            ln2["sx"][p] = S.op("tensor",
                 MM(ps_sxO[:, hf * 512:(hf + 1) * 512], ones_c[:],
                    tmpc[p][:, hf * 512:(hf + 1) * 512], c == 0, c == KC - 1),
                 waits=[("wq", nw)], inc="pe")
        for hf in range(2):
            ln2["red"][p] = S.op("tensor",
                 MM(ps_sqO[:, hf * 512:(hf + 1) * 512], ones_c[:],
                    sqt[p][:, hf * 512:(hf + 1) * 512], c == 0, c == KC - 1),
                 waits=[("act", ln2["sq"][p])], inc="pe")

    qp_v = qp[:].rearrange("p (h t) -> p t h", h=H, t=T)
    kp_v = kp[:].rearrange("p (j t) -> p t j", j=J, t=T)
    A_v = A_sb[:].rearrange("p (j b t) -> p b t j", j=J, b=T // 64, t=64)
    aevac = [0, 0]
    for blk in range(T // 64):
        a = blk % 2
        waits = [("act", n_feats), ("dve", aevac[a])]
        last_mm = 0
        for t64 in range(64):
            t = blk * 64 + t64
            last_mm = S.op("tensor", lambda e, a=a, t=t, t64=t64: e.matmul(
                    ps_A[a][:, t64 * 8:(t64 + 1) * 8], qp_v[:, t, :],
                    kp_v[:, t, :], start=True, stop=True,
                    skip_group_check=True),
                 waits=waits, inc="pe")
            waits = ()
        aevac[a] = S.op("vector", lambda e, a=a, blk=blk: e.tensor_scalar(
                A_v[:, blk], ps_A[a][:].rearrange("p (t j) -> p t j", t=64, j=J),
                2.0 / 128.0, None, ALU.mult),
             waits=[("pe", last_mm)], inc="dve")
        emit_ln2_stats(blk)
    n_A = max(aevac)
    v_m = S.op("vector", lambda e: e.tensor_scalar(row[0], ps_sxO[:], 1.0 / DM,
                                                   None, ALU.mult),
               waits=[("pe", S.n("pe"))], inc="dve")
    S.op("vector", lambda e: e.tensor_tensor(row[1], row[0], row[0], ALU.mult),
         inc="dve")
    v_mbf = S.op("vector", lambda e: e.tensor_copy(brow[1][:], row[0]), inc="dve")
    v_var = S.op("vector", lambda e: e.scalar_tensor_tensor(
                    row[2], ps_sqO[:], 1.0 / DM, row[1], ALU.mult, ALU.subtract),
                 inc="dve")
    v_eps = S.op("vector", lambda e: e.tensor_scalar(row[1], row[2], 1e-5,
                                                     None, ALU.add), inc="dve")
    a_ln = S.op("scalar", lambda e: e.activation(row[2], row[1], AF.Ln),
                waits=[("dve", v_eps)], inc="act")
    v_hf = S.op("vector", lambda e: e.tensor_scalar(row[1], row[2], -0.5,
                                                    None, ALU.mult),
                waits=[("act", a_ln)], inc="dve")
    a_rstd = S.op("scalar", lambda e: e.activation(brow[0][:], row[1], AF.Exp),
                  waits=[("dve", v_hf)], inc="act")
    if _UPTO == "ST1":
        return finish(dict(locals()))
    psf("ps_sqO"); psf("ps_sxO")
    psf("ps_A1"); psf("ps_A0")
    sbf("kp"); sbf("qp")
    n_apply = 0

    def emit_ln2_apply(c):
        nonlocal n_apply
        w = [("dve", ln2["subs"][0])]
        if c == 0:
            w.append(("pe", n_pe_st1))
        S.dma("sync", xOst0[:], xO_v[c], waits=w, inc="xq")
        nw = S.n("xq")
        ln2["subs"][0] = S.op("vector", lambda e: e.tensor_tensor(
                tmpc[2][:], xOst0[:], ps_b1O[:], ALU.subtract),
             waits=[("xq", nw), ("pe", n_bcO)], inc="dve")
        n_apply = S.op("vector", lambda e, c=c: e.tensor_tensor(
                xc(xn_s, c), tmpc[2][:], ps_b2O[:], ALU.mult), inc="dve")

    sel_bf = sba("sel_bf", [16, 16 * 128], BF16, side="right")
    S.dma("gpsimd", sel_bf[:], d["sel16"],
          waits=[("pe", S.n("pe")), ("dve", n_A)])
    n_selw = S.n("wq")
    o_sb = sba("o_sb", [128, KC * T], BF16)
    attn_v = attn[:].rearrange("p (h w k) -> p h k w", h=H, w=64, k=16)
    t8s = sba("t8s", [128, 4 * T], BF16, side="right")
    xOst0 = sba("xOst0", [128, T], F32, side="right")
    t8 = [t8s[:, i * T:(i + 1) * T] for i in range(4)]
    t4 = [tmpf[0], tmpf[1], tmpf[2]]
    ps_abc = [psa(f"ps_abc{i}", [128, T]) for i in range(2)]
    ps_b1O = psa("ps_b1O", [128, T])
    ps_b2O = psa("ps_b2O", [128, T])
    for hf in range(2):
        S.op("tensor", MM(ps_b1O[:, hf * 512:(hf + 1) * 512], ones_r[:],
                          brow[1][:, hf * 512:(hf + 1) * 512], True, True),
             waits=[("dve", v_var)], inc="pe")
    n_bcO = 0
    for hf in range(2):
        n_bcO = S.op("tensor", MM(ps_b2O[:, hf * 512:(hf + 1) * 512], ones_r[:],
                                  brow[0][:, hf * 512:(hf + 1) * 512], True, True),
             waits=[("act", a_rstd)], inc="pe")
    ln2["subs"][0] = hstate[23]["sqm"]
    ln2["subs"][1] = hstate[23]["sqm"]
    n_pe_st1 = S.n("pe")
    ph = []
    wo_free = list(wqkv_free)
    o_evac = [[0, 0], [0, 0]]
    fin_of = [0] * H

    wo_nw = {}

    def emit_oprj_dma(m):
        wb = m % 3
        S.dma("gpsimd", wv[wb], d["wo"][m], waits=[("pe", wo_free[wb])])
        wo_nw[m] = S.n("wq")

    def emit_oprj(part, m):
        sl = m % 2
        wb = m % 3
        if m not in wo_nw:
            emit_oprj_dma(m)
        n_w = wo_nw.pop(m)
        waits = [("wq", n_w), ("act", o_evac[part][sl]),
                 ("dve", fin_of[7 if part == 0 else 15]),
                 ("dve", n_apply)]
        last_mm = 0
        for kc in range(KC):
            last_mm = S.op("tensor",
                MM(ph[sl][:], wv[wb][:, kc * 128:(kc + 1) * 128],
                   attn_v[:, part * 8:(part + 1) * 8, kc, :],
                   kc == 0, kc == KC - 1),
                waits=waits, inc="pe")
            waits = ()
        wo_free[wb] = last_mm
        o_evac[part][sl] = S.op("scalar",
            lambda e, ppv=ph[sl][:], m=m, part=part: e.activation(
                xc(o_sb, m)[:, part * 512:(part + 1) * 512], ppv, AF.Identity,
                bias=bo_sb[:, m:m + 1], scale=1.0),
            waits=[("pe", last_mm)], inc="act")

    abc_free = [n_A, n_A]
    prev_fin = 0
    prev_fin_pool = 0
    for h in range(H):
        def bmult(j, slot, extra_dve, pool_gate=0):
            a = j % 2
            bc = 0
            for hf in range(2):
                bc = S.op("tensor", lambda e, a=a, h=h, j=j, hf=hf: e.matmul(
                        ps_abc[a][:, hf * 512:(hf + 1) * 512],
                        sel_bf[:, h * 128:(h + 1) * 128],
                        A_sb[:, j * T + hf * 512:j * T + (hf + 1) * 512],
                        start=True, stop=True, skip_group_check=True),
                     waits=[("dve", abc_free[a]), ("act", n_v_evac),
                            ("wq", n_selw)], inc="pe")
            md = S.op("vector", lambda e, a=a, j=j, slot=slot: e.tensor_tensor(
                    t8[slot][:], xc(v_all, j), ps_abc[a][:], ALU.mult),
                 waits=[("pe", bc), ("dve", extra_dve),
                        ("pool", max(prev_fin_pool, pool_gate))], inc="dve")
            abc_free[a] = md
            return md
        m0 = bmult(0, 0, prev_fin)
        m1 = bmult(1, 1, prev_fin)
        s01 = S.op("gpsimd", lambda e: e.tensor_tensor(t4[0][:], t8[0][:],
                                                       t8[1][:], ALU.add),
                   waits=[("dve", m1)], inc="pool")
        m2 = bmult(2, 2, prev_fin)
        m3 = bmult(3, 3, prev_fin)
        s23 = S.op("gpsimd", lambda e: e.tensor_tensor(t4[1][:], t8[2][:],
                                                       t8[3][:], ALU.add),
                   waits=[("dve", m3)], inc="pool")
        m4 = bmult(4, 0, 0, pool_gate=s01)
        m5 = bmult(5, 1, 0, pool_gate=s01)
        s03 = S.op("vector", lambda e: e.tensor_tensor(t4[0][:], t4[0][:],
                                                       t4[1][:], ALU.add),
                   waits=[("pool", s23), ("pool", s01)], inc="dve")
        s45 = S.op("vector", lambda e: e.tensor_tensor(t4[1][:], t8[0][:],
                                                       t8[1][:], ALU.add),
                   waits=[("dve", m5)], inc="dve")
        m6 = bmult(6, 2, s03)
        m7 = bmult(7, 3, s03)
        s67 = S.op("gpsimd", lambda e: e.tensor_tensor(t4[2][:], t8[2][:],
                                                       t8[3][:], ALU.add),
                   waits=[("dve", m7)], inc="pool")
        s47 = S.op("gpsimd", lambda e: e.tensor_tensor(t4[1][:], t4[1][:],
                                                       t4[2][:], ALU.add),
                   waits=[("dve", s45)], inc="pool")
        prev_fin = S.op("vector", lambda e, h=h: e.tensor_tensor(
                xc(attn, h), t4[0][:], t4[1][:], ALU.add),
             waits=[("dve", s03), ("pool", s47)], inc="dve")
        prev_fin_pool = s47
        fin_of[h] = prev_fin
        if h < 8:
            emit_ln2_apply(2 * h)
            emit_ln2_apply(2 * h + 1)
        if h == 7:
            psf("ps_b2O"); psf("ps_b1O")
            ph.append(psa("phA", [128, 512]))
            ph.append(psa("phB", [128, 512]))
        if h == 5:
            for mpre in range(3):
                emit_oprj_dma(mpre)
        if h >= 8:
            for mo in (2 * (h - 8), 2 * (h - 8) + 1):
                emit_oprj(0, mo)
                if mo + 3 < KC:
                    emit_oprj_dma(mo + 3)
    n_attn = prev_fin
    if _UPTO == "ST2":
        return finish(dict(locals()))

    sbf("xOst0")
    corr_sb = sba("corr_sb", [128, T], BF16)
    corrh_sb = sba("corrh_sb", [128, T], BF16)
    ps_S = psa("ps_S", [1, T])
    for c in range(KC):
        for hf in range(2):
            S.op("tensor", MM(ps_S[:, hf * 512:(hf + 1) * 512], ones_c[:],
                              xc(xn_s, c)[:, hf * 512:(hf + 1) * 512],
                              c == 0, c == KC - 1),
                 waits=[("dve", n_apply)], inc="pe")
    n_S = S.n("pe")
    v_s128 = S.op("vector", lambda e: e.tensor_scalar(brow[1][:], ps_S[:],
                                                      128.0, None, ALU.mult),
                  waits=[("pe", n_S)], inc="dve")
    psf("ps_S")
    ps_bcC = psa("ps_bcC", [128, T])
    for hf in range(2):
        S.op("tensor", MM(ps_bcC[:, hf * 512:(hf + 1) * 512], ones_r[:],
                          brow[1][:, hf * 512:(hf + 1) * 512], True, True),
             waits=[("dve", v_s128)], inc="pe")
    n_cb = S.n("pe")
    a_corr = S.op("scalar", lambda e: e.activation(corr_sb[:], ps_bcC[:], AF.Copy),
                  waits=[("pe", n_cb)], inc="act")
    psf("ps_bcC")
    for mpre in range(3):
        emit_oprj_dma(mpre)
    for m in range(KC):
        emit_oprj(1, m)
        if m + 3 < KC:
            emit_oprj_dma(m + 3)
    n_oall = max(o_evac[1])
    psf("phB"); psf("phA")
    psf("ps_abc1"); psf("ps_abc0")
    sbf("t8s"); sbf("sel_bf"); sbf("v_all")
    sbf("attn"); sbf("A_sb"); sbf("sel8_bf"); sbf("ind8_bf")
    sbf("rn8_bf"); sbf("c_a2"); sbf("c_a1"); sbf("sqt2")
    sbf("tmpb3"); sbf("sin_bf"); sbf("cos_bf")
    if _UPTO == "OPRJ":
        return finish(dict(locals()))

    h_sb = sba("h_sb", [128, FC * T], BF16, side="right")
    for nm in ("ppGA", "ppGB", "ppGC"):
        pp.append(psa(nm, [128, T]))
    ps_Sh = psa("ps_Sh", [1, T])
    pp_free = [0, 0, 0]
    w_free = [wo_free[0], wo_free[1], wo_free[2]]
    silu_done = 0
    h_done = 0
    last_mm = 0
    sh_pend = []
    for b in range(2 * FC):
        c, is_up = b // 2, b % 2
        sl = b % 3
        wb = b % 3
        S.dma("gpsimd", wv[wb], d["wgu"][b], waits=[("pe", w_free[wb])])
        n_w = S.n("wq")
        waits = [("wq", n_w), ("dve", pp_free[sl]), ("dve", n_attn),
                 ("act", a_corr), ("act", n_oall)]
        for kc in range(KC):
            for hf in range(2):
                last_mm = S.op("tensor",
                    MM(pp[sl][:, hf * 512:(hf + 1) * 512],
                       wv[wb][:, kc * 128:(kc + 1) * 128],
                       xc(xn_s, kc)[:, hf * 512:(hf + 1) * 512],
                       kc == 0, kc == KC - 1),
                    waits=waits, inc="pe")
                waits = ()
        w_free[wb] = last_mm
        if not is_up:
            g_sub = S.op("vector", lambda e, ppv=pp[sl][:]: e.tensor_tensor(
                    tmpf[0][:], ppv, corr_sb[:], ALU.subtract),
                 waits=[("pe", last_mm), ("act", silu_done)], inc="dve")
            pp_free[sl] = g_sub
            silu_done = S.op("scalar", lambda e, c=c: e.activation(
                    tmpf[1][:], tmpf[0][:], AF.Silu, bias=0.0,
                    scale=sg_sb[:, c:c + 1]),
                 waits=[("dve", g_sub)], inc="act")
        else:
            u_sub = S.op("vector", lambda e, ppv=pp[sl][:]: e.tensor_tensor(
                    tmpf[2][:], ppv, corr_sb[:], ALU.subtract),
                 waits=[("pe", last_mm)], inc="dve")
            pp_free[sl] = u_sub
            h_done = S.op("vector", lambda e, c=c: e.scalar_tensor_tensor(
                    xc(h_sb, c), tmpf[2][:], su_sb[:, c:c + 1], tmpf[1][:],
                    ALU.mult, ALU.mult),
                 waits=[("act", silu_done)], inc="dve")
            sh_pend.append((c, h_done))
        while len(sh_pend) > (1 if b < 2 * FC - 1 else 0):
            cc, hd = sh_pend.pop(0)
            for hf in range(2):
                S.op("tensor", MM(ps_Sh[:, hf * 512:(hf + 1) * 512], ones_c[:],
                                  xc(h_sb, cc)[:, hf * 512:(hf + 1) * 512],
                                  cc == 0, cc == FC - 1),
                     waits=[("dve", hd)], inc="pe")
    n_Sh = S.n("pe")
    n_hall = h_done
    v_sh = S.op("vector", lambda e: e.tensor_scalar(brow[1][:], ps_Sh[:],
                                                    128.0, None, ALU.mult),
                waits=[("pe", n_Sh)], inc="dve")
    for hf in range(2):
        S.op("tensor", MM(pp[0][:, hf * 512:(hf + 1) * 512], ones_r[:],
                          brow[1][:, hf * 512:(hf + 1) * 512], True, True),
             waits=[("dve", v_sh), ("dve", pp_free[0])], inc="pe")
    n_chbc = S.n("pe")
    a_corrh = S.op("scalar", lambda e, ppv=pp[0][:]: e.activation(corrh_sb[:], ppv, AF.Copy),
                   waits=[("pe", n_chbc)], inc="act")
    psf("ps_Sh")
    if _UPTO == "GUP":
        return finish(dict(locals()))

    gtmp = sba("gtmp", [128, T], BF16, side="right")
    xf = [sba(f"xf{i}", [128, T], F32, side="right") for i in range(2)]
    yT_v = yT.rearrange("(c p) t -> c p t", c=KC, p=128)
    xO_v = d["xO"].rearrange("(c p) t -> c p t", c=KC, p=128)
    wd_free = [S.n("pe"), S.n("pe")]
    g_done = 0
    pp_free3 = [S.n("dve")] * 3
    xf_free = [n_hall, n_hall]
    out_free = [0, 0]
    last_r = 0
    for m in range(KC):
        sl = m % 3
        S.dma("gpsimd", wdv[0], d["wd"][m][:, 0:2816],
              waits=[("pe", wd_free[0])])
        n_w0 = S.n("wq")
        S.dma("gpsimd", wdv[1], d["wd"][m][:, 2816:5632],
              waits=[("pe", wd_free[1])])
        n_w1 = S.n("wq")
        S.dma("sync", xf[m % 2][:], xO_v[m], waits=[("dve", xf_free[m % 2])],
              inc="xq")
        n_x = S.n("xq")
        waits = [("wq", n_w0), ("dve", pp_free3[sl]), ("dve", n_hall),
                 ("act", a_corrh)]
        last_mm = 0
        for kc in range(FC):
            if kc == 22:
                wd_free[0] = last_mm
                waits = [("wq", n_w1)]
            wsrc = (wdv[0][:, kc * 128:(kc + 1) * 128] if kc < 22 else
                    wdv[1][:, (kc - 22) * 128:(kc - 21) * 128])
            for hf in range(2):
                last_mm = S.op("tensor",
                    MM(pp[sl][:, hf * 512:(hf + 1) * 512], wsrc,
                       xc(h_sb, kc)[:, hf * 512:(hf + 1) * 512],
                       kc == 0, kc == FC - 1),
                    waits=waits, inc="pe")
                waits = ()
        wd_free[1] = last_mm
        d_sub = S.op("vector", lambda e, ppv=pp[sl][:]: e.tensor_tensor(
                tmpf[0][:], ppv, corrh_sb[:], ALU.subtract),
             waits=[("pe", last_mm), ("act", g_done), ("dve", last_r)],
             inc="dve")
        pp_free3[sl] = d_sub
        g_done = S.op("scalar", lambda e, m=m: e.activation(
                gtmp[:], tmpf[0][:], AF.Silu, bias=0.0,
                scale=sd_sb[:, m:m + 1]),
             waits=[("dve", d_sub)], inc="act")
        og = S.op("vector", lambda e, m=m: e.tensor_tensor(
                tmpf[0][:], gtmp[:], xc(o_sb, m), ALU.mult),
             waits=[("act", g_done), ("act", n_oall)], inc="dve")
        last_r = S.op("vector", lambda e, m=m: e.tensor_tensor(
                tmpf[1 + (m % 2)][:], tmpf[0][:], xf[m % 2][:], ALU.add),
             waits=[("xq", n_x), ("st", out_free[m % 2])], inc="dve")
        xf_free[m % 2] = last_r
        out_free[m % 2] = S.op("sync", lambda e, m=m: e.dma_start(
                yT_v[m], tmpf[1 + (m % 2)][:]),
             waits=[("dve", last_r)], inc="st", amt=16)

    return finish(dict(locals()))


_NC_CACHE = {}


def kernel(**inputs):
    per_core = _prep_host(inputs)
    key = ("prog", _UPTO, tuple(_DEBUG_TAPS))
    if key not in _NC_CACHE:
        _NC_CACHE.clear()
        _NC_CACHE[key] = build_program()
    nc = _NC_CACHE[key]
    res = run_bass_kernel_spmd(nc, per_core, core_ids=list(range(NCORES)))
    outs = [np.asarray(r["yT"], dtype=np.float32) for r in res.results]
    y = np.zeros((4, 2048, DM), dtype=np.float32)
    for c in range(NCORES):
        b, half = c // 2, c % 2
        y[b, _out_cols(half), :] = outs[c].T
    if _DEBUG_TAPS:
        kernel._taps = [{k: np.asarray(v) for k, v in r.items()
                         if k.startswith("tap_")} for r in res.results]
    return y


# version 19 (final)
# speedup vs baseline: 1.2583x; 1.0070x over previous
import numpy as np
from contextlib import ExitStack

import concourse.bass as bass
import concourse.mybir as mybir
from concourse.bass_utils import run_bass_kernel_spmd

F32 = mybir.dt.float32
BF16 = mybir.dt.bfloat16
AF = mybir.ActivationFunctionType
ALU = mybir.AluOpType

NCORES = 8
T = 1024
KC = 16
DM = 2048
FC = 44
H = 16
J = 8
E = 64
PI = float(np.pi)
TWO_PI = float(2 * np.pi)
HALF_PI = float(np.pi / 2)

_DEBUG_TAPS = []
_UPTO = None


class Sched:
    ENGINES = ("sync", "gpsimd", "scalar", "vector", "tensor")

    def __init__(self, nc):
        self.nc = nc
        self.streams = {e: [] for e in self.ENGINES}
        self.sems = {}
        self.counts = {}

    def sem(self, name, stack):
        s = stack.enter_context(self.nc.semaphore(name))
        self.sems[name] = s
        self.counts[name] = 0
        return s

    def n(self, name):
        return self.counts[name]

    def op(self, eng, fn, waits=(), inc=None, amt=1):
        w = tuple((s, c) for s, c in waits if c > 0)
        self.streams[eng].append((fn, w, inc, amt))
        if inc is not None:
            self.counts[inc] += amt
            return self.counts[inc]
        return None

    def dma(self, eng, out_ap, in_ap, waits=(), inc="wq"):
        def fn(e, out_ap=out_ap, in_ap=in_ap):
            return e.dma_start(out_ap, in_ap)
        return self.op(eng, fn, waits, inc, 16)

    def realize(self, block):
        for eng in self.ENGINES:
            ops = self.streams[eng]
            if not ops:
                continue
            dec = getattr(block, eng)

            def build(e, ops=ops):
                waited = {}
                for fn, waits, inc, amt in ops:
                    for sname, cnt in waits:
                        if waited.get(sname, 0) < cnt:
                            e.wait_ge(self.sems[sname], cnt)
                            waited[sname] = cnt
                    inst = fn(e)
                    if inc is not None and inst is not None:
                        inst.then_inc(self.sems[inc], amt)
            dec(build)


def _blocks(w, kc):
    mo = w.shape[0] // 128
    a = w.reshape(mo, 128, kc, 128).transpose(0, 3, 2, 1)
    return np.ascontiguousarray(a.reshape(mo, 128, kc * 128))


def _out_cols(half):
    r = np.arange(T)
    return (r // 64) * 128 + 64 * half + (r % 64)


def _prep_host(inp):
    x = np.asarray(inp["L0_attn_hidden_states"], dtype=np.float32)

    wqkv = np.concatenate([
        _blocks(np.asarray(inp["L0_attn_Wq"], np.float32), KC),
        _blocks(np.asarray(inp["L0_attn_Wk"], np.float32), KC),
        _blocks(np.asarray(inp["L0_attn_Wv"], np.float32), KC),
    ], axis=0)
    wo = _blocks(np.asarray(inp["L0_attn_Wo"], np.float32), KC)
    wg = _blocks(np.asarray(inp["L0_ffn_wq_gate"], np.int32).astype(np.float32), KC)
    wu = _blocks(np.asarray(inp["L0_ffn_wq_up"], np.int32).astype(np.float32), KC)
    wgu = np.ascontiguousarray(
        np.stack([wg, wu], axis=1).reshape(2 * FC, 128, KC * 128))
    wd = _blocks(np.asarray(inp["L0_ffn_wq_down"], np.int32).astype(np.float32), FC)

    bqkv = np.ascontiguousarray(np.concatenate([
        np.asarray(inp["L0_attn_bq"], np.float32).reshape(H, 128).T,
        np.asarray(inp["L0_attn_bk"], np.float32).reshape(J, 128).T,
        np.asarray(inp["L0_attn_bv"], np.float32).reshape(J, 128).T,
    ], axis=1))
    bo = np.ascontiguousarray(np.asarray(inp["L0_attn_bo"], np.float32).reshape(KC, 128).T)
    sg = np.ascontiguousarray(np.asarray(inp["L0_ffn_s_gate"], np.float32).reshape(FC, 128).T)
    su = np.ascontiguousarray(np.asarray(inp["L0_ffn_s_up"], np.float32).reshape(FC, 128).T)
    sd = np.ascontiguousarray(np.asarray(inp["L0_ffn_s_down"], np.float32).reshape(KC, 128).T)
    Rt = np.ascontiguousarray(np.asarray(inp["L0_attn_rand_matrix"], np.float32).T)

    inv = (1.0 / (10000.0 ** (np.arange(0, 64, dtype=np.float64) * 2.0 / 128.0)))
    sel16 = np.zeros((16, 16 * 128), dtype=np.float32)
    for h in range(16):
        sel16[h, h * 128:(h + 1) * 128] = 1.0
    ind8 = np.zeros((128, 8 * 8), dtype=np.float32)
    for g in range(8):
        ind8[:, g * 8 + g] = 1.0
    sel8 = np.zeros((8, 8 * 128), dtype=np.float32)
    for g in range(8):
        sel8[g, g * 128:(g + 1) * 128] = 1.0

    per_core = []
    for c in range(NCORES):
        b, half = c // 2, c % 2
        xbT = np.ascontiguousarray(x[b].T)
        src = xbT[:, half * T:(half + 1) * T]
        out = xbT[:, _out_cols(half)]
        pos = half * T + np.arange(T)
        ang = pos[None, :].astype(np.float64) * inv[np.arange(128) % 64][:, None]
        cos_t = np.cos(ang).astype(np.float32)
        sin_t = np.sin(ang).astype(np.float32)
        sin_t[:64, :] *= -1.0
        per_core.append({
            "xS": np.ascontiguousarray(src), "xO": np.ascontiguousarray(out),
            "wqkv": wqkv, "wo": wo, "wgu": wgu, "wd": wd,
            "bqkv": bqkv, "bo": bo, "sg": sg, "su": su, "sd": sd,
            "Rt": Rt, "cosT": cos_t, "sinT": sin_t, "sel16": sel16,
            "ind8": ind8, "sel8": sel8,
        })
    return per_core


def build_program():
    nc = bass.Bass("TRN2", target_bir_lowering=False)
    stack = ExitStack()

    d = {}
    def din(name, shape):
        d[name] = nc.dram_tensor(name, list(shape), F32, kind="ExternalInput").ap()
    din("xS", [DM, T])
    din("xO", [DM, T])
    din("wqkv", [32, 128, KC * 128])
    din("wo", [KC, 128, KC * 128])
    din("wgu", [2 * FC, 128, KC * 128])
    din("wd", [KC, 128, FC * 128])
    din("bqkv", [128, 32])
    din("bo", [128, KC])
    din("sg", [128, FC])
    din("su", [128, FC])
    din("sd", [128, KC])
    din("Rt", [128, E])
    din("cosT", [128, T])
    din("sinT", [128, T])
    din("sel16", [16, 16 * 128])
    din("ind8", [128, 8 * 8])
    din("sel8", [8, 8 * 128])
    yT = nc.dram_tensor("yT", [DM, T], F32, kind="ExternalOutput").ap()

    S = Sched(nc)
    for s in ("wq", "xq", "st", "pe", "act", "dve", "pool"):
        S.sem(s, stack)

    def finish(allv):
        if _DEBUG_TAPS:
            fin_waits = [("pe", S.n("pe")), ("dve", S.n("dve")),
                         ("act", S.n("act")), ("pool", S.n("pool")),
                         ("xq", S.n("xq")), ("wq", S.n("wq"))]
            for name in _DEBUG_TAPS:
                if name not in allv:
                    continue
                tsb = allv[name]
                ap = nc.dram_tensor("tap_" + name, list(tsb[:].shape), F32,
                                    kind="ExternalOutput").ap()
                S.dma("gpsimd", ap, tsb[:], waits=fin_waits, inc="st")
        S.op("sync", lambda e: e.nop(), waits=[("st", S.n("st"))])
        block = stack.enter_context(nc.Block())
        S.realize(block)
        stack.close()
        return nc

    sb_guards = {}
    def sba(name, shape, dt, side="left"):
        g = nc.sbuf_tensor(name, list(shape), dt, side=side)
        t = g.__enter__()
        sb_guards[name] = g
        return t
    def sbf(name):
        sb_guards.pop(name).__exit__(None, None, None)
    ps_guards = {}
    def psa(name, shape):
        g = nc.psum_tensor(name, list(shape), F32)
        t = g.__enter__()
        ps_guards[name] = g
        return t
    def psf(name):
        ps_guards.pop(name).__exit__(None, None, None)

    Rt_bf = sba("Rt_bf", [128, E], BF16)
    bqkv_sb = sba("bqkv_sb", [128, 32], F32)
    bo_sb = sba("bo_sb", [128, KC], F32)
    sg_sb = sba("sg_sb", [128, FC], F32)
    su_sb = sba("su_sb", [128, FC], F32)
    sd_sb = sba("sd_sb", [128, KC], F32)
    ones_c = sba("ones_c", [128, 1], BF16)
    ones_r = sba("ones_r", [1, 128], BF16)
    sinbias = sba("sinbias", [128, 1], F32)
    wpool = sba("wpool", [128, 6144], BF16)
    wv = [wpool[:, i * 2048:(i + 1) * 2048] for i in range(3)]
    wdv = [wpool[:, i * 2816:(i + 1) * 2816] for i in range(2)]
    tmpf = [sba(f"tmpf{i}", [128, T], F32) for i in range(3)]
    rows = sba("rows", [1, 3 * T], F32)
    row = [rows[:, i * T:(i + 1) * T] for i in range(3)]
    browt = sba("browt", [1, 2 * T], BF16)
    brow = [browt[:, i * T:(i + 1) * T] for i in range(2)]

    def xc(t, c):
        return t[:, c * T:(c + 1) * T]

    def MM(out, l, r, st_, sp):
        return lambda e: e.matmul(out, l, r, start=st_, stop=sp,
                                  skip_group_check=True)

    S.dma("gpsimd", Rt_bf[:], d["Rt"])
    for dst, src in ((bqkv_sb, "bqkv"), (bo_sb, "bo"), (sg_sb, "sg"),
                     (su_sb, "su"), (sd_sb, "sd")):
        S.dma("sync", dst[:], d[src], inc="xq")
    n_cxq = S.n("xq")
    S.op("vector", lambda e: e.memset(ones_c[:], 1.0), inc="dve")
    S.op("vector", lambda e: e.memset(ones_r[:], 1.0), inc="dve")
    S.op("vector", lambda e: e.memset(sinbias[0:64, :], 0.0), inc="dve")
    n_memset = S.op("vector", lambda e: e.memset(sinbias[64:128, :], -HALF_PI),
                    inc="dve")

    def emit_ln(tag, x_dram, xn_t, gate_waits):
        xbf = sba("xbf" + tag, [128, KC * T], BF16, side="right")
        tb = sba("tb" + tag, [128, 2 * T], BF16, side="right")
        sq_t = [tb[:, 0:T], tb[:, T:2 * T]]
        xv = xbf[:].rearrange("p (c t) -> p c t", c=KC)
        dv_ = x_dram.rearrange("(c p) t -> p c t", c=KC, p=128)
        n_xh = []
        for qtr in range(8):
            S.dma("gpsimd", xv[:, qtr * 2:(qtr + 1) * 2],
                  dv_[:, qtr * 2:(qtr + 1) * 2], waits=gate_waits)
            gate_waits = ()
            n_xh.append(S.n("wq"))
        ps_sx = psa("ps_sx" + tag, [1, T])
        ps_sq = psa("ps_sq" + tag, [1, T])
        sq_done = [0, 0]
        red_done = [0, 0]
        for c in range(KC):
            p = c % 2
            n_x = n_xh[c // 2]
            w_sq = [("wq", n_x)] + ([("pe", red_done[p])] if c >= 2 else [])
            sq_done[p] = S.op("scalar",
                lambda e, c=c, p=p: e.activation(sq_t[p][:], xc(xbf, c), AF.Square),
                waits=w_sq, inc="act")
            for hf in range(2):
                S.op("tensor", MM(ps_sx[:, hf * 512:(hf + 1) * 512], ones_c[:],
                                  xc(xbf, c)[:, hf * 512:(hf + 1) * 512],
                                  c == 0, c == KC - 1),
                     waits=[("wq", n_x), ("dve", n_memset)], inc="pe")
            for hf in range(2):
                red_done[p] = S.op("tensor",
                     MM(ps_sq[:, hf * 512:(hf + 1) * 512], ones_c[:],
                        sq_t[p][:, hf * 512:(hf + 1) * 512], c == 0, c == KC - 1),
                     waits=[("act", sq_done[p])], inc="pe")
        n_red = S.n("pe")
        v_m = S.op("vector", lambda e: e.tensor_scalar(row[0], ps_sx[:], 1.0 / DM,
                                                       None, ALU.mult),
                   waits=[("pe", n_red), ("xq", n_cxq)], inc="dve")
        S.op("vector", lambda e: e.tensor_tensor(row[1], row[0], row[0], ALU.mult),
             inc="dve")
        v_mbf = S.op("vector", lambda e: e.tensor_copy(brow[1][:], row[0]),
                     inc="dve")
        v_var = S.op("vector", lambda e: e.scalar_tensor_tensor(
                        row[2], ps_sq[:], 1.0 / DM, row[1], ALU.mult, ALU.subtract),
                     inc="dve")
        v_eps = S.op("vector", lambda e: e.tensor_scalar(row[1], row[2], 1e-5,
                                                         None, ALU.add), inc="dve")
        a_ln = S.op("scalar", lambda e: e.activation(row[2], row[1], AF.Ln),
                    waits=[("dve", v_eps), ("xq", n_cxq)], inc="act")
        v_hf = S.op("vector", lambda e: e.tensor_scalar(row[1], row[2], -0.5,
                                                        None, ALU.mult),
                    waits=[("act", a_ln)], inc="dve")
        a_rstd = S.op("scalar", lambda e: e.activation(brow[0][:], row[1], AF.Exp),
                      waits=[("dve", v_hf)], inc="act")
        psf("ps_sq" + tag); psf("ps_sx" + tag)
        ps_b1 = psa("ps_b1" + tag, [128, T])
        ps_b2 = psa("ps_b2" + tag, [128, T])
        for hf in range(2):
            S.op("tensor", MM(ps_b1[:, hf * 512:(hf + 1) * 512], ones_r[:],
                              brow[1][:, hf * 512:(hf + 1) * 512], True, True),
                 waits=[("dve", v_var)], inc="pe")
        for hf in range(2):
            S.op("tensor", MM(ps_b2[:, hf * 512:(hf + 1) * 512], ones_r[:],
                              brow[0][:, hf * 512:(hf + 1) * 512], True, True),
                 waits=[("act", a_rstd)], inc="pe")
        n_bc = S.n("pe")
        n_xn = 0
        n_xn_c = []
        for c in range(KC):
            S.op("vector", lambda e, c=c: e.tensor_tensor(
                    tmpf[c % 2][:], xc(xbf, c), ps_b1[:], ALU.subtract),
                 waits=[("pe", n_bc)], inc="dve")
            n_xn = S.op("vector", lambda e, c=c: e.tensor_tensor(
                    xc(xn_t, c), tmpf[c % 2][:], ps_b2[:], ALU.mult), inc="dve")
            n_xn_c.append(n_xn)
        psf("ps_b2" + tag); psf("ps_b1" + tag)
        sbf("tb" + tag); sbf("xbf" + tag)
        return n_xn, n_xn_c

    xn_s = sba("xn_s", [128, KC * T], BF16, side="right")
    n_xn, n_xn_c = emit_ln("S", d["xS"], xn_s, [])
    if _UPTO == "LN1":
        allv = dict(locals()); allv["xn"] = xn_s
        return finish(allv)

    cos_bf = sba("cos_bf", [128, T], BF16, side="right")
    sin_bf = sba("sin_bf", [128, T], BF16, side="right")
    tmpb3 = sba("tmpb3", [128, 3 * T], BF16, side="right")
    tmpc = [tmpb3[:, i * T:(i + 1) * T] for i in range(3)]
    sqt2 = sba("sqt2", [128, 2 * T], BF16, side="right")
    sqt = [sqt2[:, 0:T], sqt2[:, T:2 * T]]
    c_a1 = sba("c_a1", [128, T], BF16, side="right")
    c_a2 = sba("c_a2", [128, T], BF16, side="right")
    rn8_bf = sba("rn8_bf", [8, T], BF16, side="right")
    ind8_bf = sba("ind8_bf", [128, 64], BF16, side="right")
    sel8_bf = sba("sel8_bf", [8, 8 * 128], BF16, side="right")
    A_sb = sba("A_sb", [16, J * T], BF16, side="right")
    attn = sba("attn", [128, H * T], BF16, side="right")
    v_all = sba("v_all", [128, J * T], BF16, side="right")
    qp = sba("qp", [128, H * T], BF16, side="right")
    kp = sba("kp", [128, J * T], BF16, side="right")
    S.dma("gpsimd", cos_bf[:], d["cosT"], waits=[("dve", n_xn)])
    S.dma("gpsimd", sin_bf[:], d["sinT"])
    S.dma("gpsimd", ind8_bf[:], d["ind8"])
    S.dma("gpsimd", sel8_bf[:], d["sel8"])
    n_tables = S.n("wq")

    pp = [psa(f"pp{i}", [128, T]) for i in range(2)]
    ps_ss8 = psa("ps_ss8", [8, T])
    ps_sh = psa("ps_sh", [128, T])
    wqkv_free = [0, 0, 0]
    pp_evac = [0, 0]
    pp_free4 = [0, 0]
    def emit_blk(bi, m, mid=None):
        sl, wb = bi % 2, bi % 3
        S.dma("gpsimd", wv[wb], d["wqkv"][m], waits=[("pe", wqkv_free[wb])])
        n_w = S.n("wq")
        waits = [("wq", n_w), ("act", pp_evac[sl]), ("dve", pp_free4[sl])]
        last_mm = 0
        for kc in range(KC):
            waits = list(waits) + [("dve", n_xn_c[kc])]
            for hf in range(2):
                last_mm = S.op("tensor",
                    MM(pp[sl][:, hf * 512:(hf + 1) * 512],
                       wv[wb][:, kc * 128:(kc + 1) * 128],
                       xc(xn_s, kc)[:, hf * 512:(hf + 1) * 512],
                       kc == 0, kc == KC - 1),
                    waits=waits, inc="pe")
                waits = ()
            if kc == 7 and mid is not None:
                mid()
        wqkv_free[wb] = last_mm
        return last_mm, sl

    for bi, m in enumerate(range(24, 32)):
        last_mm, sl = emit_blk(bi, m)
        pp_evac[sl] = S.op("scalar", lambda e, ppv=pp[sl][:], m=m: e.activation(
                xc(v_all, m - 24), ppv, AF.Identity,
                bias=bqkv_sb[:, m:m + 1], scale=1.0),
             waits=[("pe", last_mm)], inc="act")
    n_v_evac = max(pp_evac)

    hstate = [dict() for _ in range(24)]
    r3_prev = 0
    st0_free = [0]
    ssmm_of = [0] * 24
    rnexp_of = [0, 0, 0]
    rnmax_of = [0, 0, 0]
    cstate = {"evp": 0, "qs": 0, "ab": 0, "sin": 0, "y2": 0}

    def emit_A_blk(i, mid=None):
        st = hstate[i]
        last_mm, sl = emit_blk(8 + i, i, mid=mid)
        st["dst"] = xc(qp, i) if i < 16 else xc(kp, i - 16)
        st["last_mm"] = last_mm
        st["sl"] = sl

    def emit_A_post(i):
        nonlocal r3_prev
        st = hstate[i]
        last_mm, sl, dst = st["last_mm"], st["sl"], st["dst"]
        ev = S.op("scalar", lambda e, ppv=pp[sl][:], i=i: e.activation(
                tmpc[0][:], ppv, AF.Identity, bias=bqkv_sb[:, i:i + 1],
                scale=1.0),
             waits=[("pe", last_mm), ("dve", r3_prev)], inc="act")
        pp_evac[sl] = ev
        r1 = S.op("vector", lambda e: e.tensor_tensor(tmpc[1][:], tmpc[0][:],
                                                      cos_bf[:], ALU.mult),
                  waits=[("act", ev), ("wq", n_tables)], inc="dve")
        pp_free4[sl] = r1
        S.op("vector", lambda e: e.tensor_copy(tmpc[2][0:64, :],
                                               tmpc[0][64:128, :]), inc="dve")
        S.op("vector", lambda e: e.tensor_copy(tmpc[2][64:128, :],
                                               tmpc[0][0:64, :]), inc="dve")
        r3 = S.op("vector", lambda e: e.tensor_tensor(tmpc[0][:], tmpc[2][:],
                                                      sin_bf[:], ALU.mult),
                  inc="dve")
        r3_prev = r3
        S.op("vector", lambda e, dst=dst: e.tensor_tensor(
                dst, tmpc[1][:], tmpc[0][:], ALU.add), inc="dve")
        st["sqm"] = S.op("vector", lambda e, dst=dst, i=i: e.tensor_tensor(
                sqt[i % 2][:], dst, dst, ALU.mult),
             waits=[("pe", ssmm_of[i - 2] if i >= 2 else 0)], inc="dve")

    def emit_ss(i):
        g, idx = i // 8, i % 8
        w = [("dve", hstate[i]["sqm"])]
        if idx == 0 and g >= 1:
            w.append(("dve", rnmax_of[g - 1]))
        last = 0
        for hf in range(2):
            last = S.op("tensor", MM(ps_ss8[:, hf * 512:(hf + 1) * 512],
                                     ind8_bf[:, idx * 8:(idx + 1) * 8],
                                     sqt[i % 2][:, hf * 512:(hf + 1) * 512],
                                     idx == 0, idx == 7),
                 waits=w, inc="pe")
            w = ()
        ssmm_of[i] = last

    def emit_B(g):
        rnmax_of[g] = S.op("vector", lambda e: e.tensor_scalar(
                tmpf[0][0:8, :], ps_ss8[:], 1e-12, None, ALU.max),
             waits=[("pe", ssmm_of[g * 8 + 7])], inc="dve")
        lnb = S.op("scalar", lambda e: e.activation(tmpf[1][0:8, :],
                                                    tmpf[0][0:8, :], AF.Ln),
                   waits=[("dve", rnmax_of[g])], inc="act")
        sc = S.op("vector", lambda e: e.tensor_scalar(
                tmpf[0][0:8, :], tmpf[1][0:8, :], -0.5, None, ALU.mult),
             waits=[("act", lnb)], inc="dve")
        rnexp_of[g] = S.op("scalar", lambda e: e.activation(
                rn8_bf[:], tmpf[0][0:8, :], AF.Exp),
             waits=[("dve", sc)], inc="act")

    tstate = {}

    def emit_C_bc(i, tile):
        g, idx = i // 8, i % 8
        ts_ = tstate.setdefault(id(tile), {"evp": 0, "qs": 0})
        w = [("act", rnexp_of[g]), ("act", ts_["evp"]), ("dve", ts_["qs"])]
        bc = 0
        for hf in range(2):
            bc = S.op("tensor", MM(tile[:, hf * 512:(hf + 1) * 512],
                                   sel8_bf[:, idx * 128:(idx + 1) * 128],
                                   rn8_bf[:, hf * 512:(hf + 1) * 512],
                                   True, True),
                 waits=w, inc="pe")
            w = ()
        hstate[i]["bc"] = bc

    def emit_C_rest(i, tile):
        dst = hstate[i]["dst"]
        ts_ = tstate.setdefault(id(tile), {"evp": 0, "qs": 0})
        qs = S.op("vector", lambda e, dst=dst, tile=tile: e.tensor_tensor(
                c_a2[:], dst, tile[:], ALU.mult),
             waits=[("pe", hstate[i]["bc"]), ("act", cstate["sin"])], inc="dve")
        ts_["qs"] = qs
        rp = 0
        w = [("dve", qs)]
        for hf in range(2):
            rp = S.op("tensor", MM(tile[0:64, hf * 512:(hf + 1) * 512], Rt_bf[:],
                                   c_a2[:, hf * 512:(hf + 1) * 512], True, True),
                 waits=w, inc="pe")
            w = ()
        evp = S.op("scalar", lambda e, tile=tile: e.activation(c_a2[0:64, :],
                                                    tile[0:64, :], AF.Copy),
                   waits=[("pe", rp), ("dve", cstate["ab"])], inc="act")
        ts_["evp"] = evp
        S.op("vector", lambda e: e.tensor_scalar(
                c_a1[0:64, :], c_a2[0:64, :], PI, TWO_PI,
                ALU.is_gt, ALU.mult), waits=[("act", evp)], inc="dve")
        S.op("vector", lambda e: e.tensor_scalar(
                c_a1[64:128, :], c_a2[0:64, :], -PI, TWO_PI,
                ALU.is_lt, ALU.mult), inc="dve")
        S.op("vector", lambda e: e.scalar_tensor_tensor(
                c_a2[64:128, :], c_a1[0:64, :], -1.0, c_a2[0:64, :],
                ALU.mult, ALU.add), inc="dve")
        S.op("vector", lambda e: e.tensor_tensor(
                c_a2[0:64, :], c_a2[64:128, :], c_a1[64:128, :], ALU.add),
             inc="dve")
        ab = S.op("vector", lambda e: e.scalar_tensor_tensor(
                c_a2[64:128, :], c_a2[0:64, :], -1.0, c_a2[0:64, :],
                ALU.mult, ALU.max), inc="dve")
        cstate["ab"] = ab
        cstate["sin"] = S.op("scalar", lambda e, dst=dst: e.activation(
                dst, c_a2[:], AF.Sin, bias=sinbias[:, 0:1], scale=1.0),
                waits=[("dve", ab)], inc="act")

    for g in range(3):
        for idx in range(8):
            i = g * 8 + idx
            mid = None
            if g >= 1:
                mid = (lambda pi=(g - 1) * 8 + idx: emit_C_bc(pi, ps_sh))
            emit_A_blk(i, mid=mid)
            if g >= 1:
                emit_C_rest((g - 1) * 8 + idx, ps_sh)
            if idx >= 1:
                emit_ss(i - 1)
            emit_A_post(i)
        emit_ss(g * 8 + 7)
        emit_B(g)
    psf("ps_sh"); psf("ps_ss8")
    sh2 = [psa("sh2a", [128, T]), psa("sh2b", [128, T])]
    emit_C_bc(16, sh2[0])
    for idx in range(8):
        if idx < 7:
            emit_C_bc(17 + idx, sh2[(idx + 1) % 2])
        emit_C_rest(16 + idx, sh2[idx % 2])

    n_feats = cstate["sin"]
    psf("sh2b"); psf("sh2a")
    psf("pp1"); psf("pp0")
    pp.clear()
    if _UPTO == "QKV":
        allv = dict(locals()); allv["xn"] = xn_s
        return finish(allv)

    ps_A = [psa(f"ps_A{i}", [16, 512]) for i in range(2)]
    ps_sxO = psa("ps_sxO", [1, T])
    ps_sqO = psa("ps_sqO", [1, T])
    xO_v = d["xO"].rearrange("(c p) t -> c p t", c=KC, p=128)
    ln2 = {"sq": [0, 0], "red": [0, 0], "sx": [0, 0], "subs": [0, 0]}

    def emit_ln2_stats(c):
        p = c % 2
        w = [("act", ln2["sq"][p]), ("pe", ln2["sx"][p])]
        if c == 0:
            w += [("dve", hstate[23]["sqm"]), ("pe", ssmm_of[23])]
        S.dma("gpsimd", tmpc[p][:], xO_v[c], waits=w)
        nw = S.n("wq")
        w2 = [("wq", nw)]
        if c <= 1:
            w2.append(("pe", ssmm_of[23]))
        else:
            w2.append(("pe", ln2["red"][p]))
        ln2["sq"][p] = S.op("scalar", lambda e, p=p: e.activation(
                sqt[p][:], tmpc[p][:], AF.Square), waits=w2, inc="act")
        for hf in range(2):
            ln2["sx"][p] = S.op("tensor",
                 MM(ps_sxO[:, hf * 512:(hf + 1) * 512], ones_c[:],
                    tmpc[p][:, hf * 512:(hf + 1) * 512], c == 0, c == KC - 1),
                 waits=[("wq", nw)], inc="pe")
        for hf in range(2):
            ln2["red"][p] = S.op("tensor",
                 MM(ps_sqO[:, hf * 512:(hf + 1) * 512], ones_c[:],
                    sqt[p][:, hf * 512:(hf + 1) * 512], c == 0, c == KC - 1),
                 waits=[("act", ln2["sq"][p])], inc="pe")

    qp_v = qp[:].rearrange("p (h t) -> p t h", h=H, t=T)
    kp_v = kp[:].rearrange("p (j t) -> p t j", j=J, t=T)
    A_v = A_sb[:].rearrange("p (j b t) -> p b t j", j=J, b=T // 64, t=64)
    aevac = [0, 0]
    for blk in range(T // 64):
        a = blk % 2
        waits = [("act", n_feats), ("dve", aevac[a])]
        last_mm = 0
        for t64 in range(64):
            t = blk * 64 + t64
            last_mm = S.op("tensor", lambda e, a=a, t=t, t64=t64: e.matmul(
                    ps_A[a][:, t64 * 8:(t64 + 1) * 8], qp_v[:, t, :],
                    kp_v[:, t, :], start=True, stop=True,
                    skip_group_check=True),
                 waits=waits, inc="pe")
            waits = ()
        aevac[a] = S.op("vector", lambda e, a=a, blk=blk: e.tensor_scalar(
                A_v[:, blk], ps_A[a][:].rearrange("p (t j) -> p t j", t=64, j=J),
                2.0 / 128.0, None, ALU.mult),
             waits=[("pe", last_mm)], inc="dve")
        emit_ln2_stats(blk)
    n_A = max(aevac)
    v_m = S.op("vector", lambda e: e.tensor_scalar(row[0], ps_sxO[:], 1.0 / DM,
                                                   None, ALU.mult),
               waits=[("pe", S.n("pe"))], inc="dve")
    S.op("vector", lambda e: e.tensor_tensor(row[1], row[0], row[0], ALU.mult),
         inc="dve")
    v_mbf = S.op("vector", lambda e: e.tensor_copy(brow[1][:], row[0]), inc="dve")
    v_var = S.op("vector", lambda e: e.scalar_tensor_tensor(
                    row[2], ps_sqO[:], 1.0 / DM, row[1], ALU.mult, ALU.subtract),
                 inc="dve")
    v_eps = S.op("vector", lambda e: e.tensor_scalar(row[1], row[2], 1e-5,
                                                     None, ALU.add), inc="dve")
    a_ln = S.op("scalar", lambda e: e.activation(row[2], row[1], AF.Ln),
                waits=[("dve", v_eps)], inc="act")
    v_hf = S.op("vector", lambda e: e.tensor_scalar(row[1], row[2], -0.5,
                                                    None, ALU.mult),
                waits=[("act", a_ln)], inc="dve")
    a_rstd = S.op("scalar", lambda e: e.activation(brow[0][:], row[1], AF.Exp),
                  waits=[("dve", v_hf)], inc="act")
    if _UPTO == "ST1":
        return finish(dict(locals()))
    psf("ps_sqO"); psf("ps_sxO")
    psf("ps_A1"); psf("ps_A0")
    sbf("kp"); sbf("qp")
    n_apply = 0

    def emit_ln2_apply(c):
        nonlocal n_apply
        w = [("dve", ln2["subs"][0])]
        if c == 0:
            w.append(("pe", n_pe_st1))
        S.dma("sync", xOst0[:], xO_v[c], waits=w, inc="xq")
        nw = S.n("xq")
        ln2["subs"][0] = S.op("vector", lambda e: e.tensor_tensor(
                tmpc[2][:], xOst0[:], ps_b1O[:], ALU.subtract),
             waits=[("xq", nw), ("pe", n_bcO)], inc="dve")
        n_apply = S.op("vector", lambda e, c=c: e.tensor_tensor(
                xc(xn_s, c), tmpc[2][:], ps_b2O[:], ALU.mult), inc="dve")

    sel_bf = sba("sel_bf", [16, 16 * 128], BF16, side="right")
    S.dma("gpsimd", sel_bf[:], d["sel16"],
          waits=[("pe", S.n("pe")), ("dve", n_A)])
    n_selw = S.n("wq")
    o_sb = sba("o_sb", [128, KC * T], BF16)
    attn_v = attn[:].rearrange("p (h w k) -> p h k w", h=H, w=64, k=16)
    t8s = sba("t8s", [128, 4 * T], BF16, side="right")
    xOst0 = sba("xOst0", [128, T], F32, side="right")
    t8 = [t8s[:, i * T:(i + 1) * T] for i in range(4)]
    t4 = [tmpf[0], tmpf[1], tmpf[2]]
    ps_abc = [psa(f"ps_abc{i}", [128, T]) for i in range(2)]
    ps_b1O = psa("ps_b1O", [128, T])
    ps_b2O = psa("ps_b2O", [128, T])
    for hf in range(2):
        S.op("tensor", MM(ps_b1O[:, hf * 512:(hf + 1) * 512], ones_r[:],
                          brow[1][:, hf * 512:(hf + 1) * 512], True, True),
             waits=[("dve", v_var)], inc="pe")
    n_bcO = 0
    for hf in range(2):
        n_bcO = S.op("tensor", MM(ps_b2O[:, hf * 512:(hf + 1) * 512], ones_r[:],
                                  brow[0][:, hf * 512:(hf + 1) * 512], True, True),
             waits=[("act", a_rstd)], inc="pe")
    ln2["subs"][0] = hstate[23]["sqm"]
    ln2["subs"][1] = hstate[23]["sqm"]
    n_pe_st1 = S.n("pe")
    ph = []
    wo_free = list(wqkv_free)
    o_evac = [[0, 0], [0, 0]]
    fin_of = [0] * H

    wo_nw = {}

    def emit_oprj_dma(m):
        wb = m % 3
        S.dma("gpsimd", wv[wb], d["wo"][m], waits=[("pe", wo_free[wb])])
        wo_nw[m] = S.n("wq")

    def emit_oprj(part, m):
        sl = m % 2
        wb = m % 3
        if m not in wo_nw:
            emit_oprj_dma(m)
        n_w = wo_nw.pop(m)
        waits = [("wq", n_w), ("act", o_evac[part][sl]),
                 ("dve", fin_of[7 if part == 0 else 15]),
                 ("dve", n_apply)]
        last_mm = 0
        for kc in range(KC):
            last_mm = S.op("tensor",
                MM(ph[sl][:], wv[wb][:, kc * 128:(kc + 1) * 128],
                   attn_v[:, part * 8:(part + 1) * 8, kc, :],
                   kc == 0, kc == KC - 1),
                waits=waits, inc="pe")
            waits = ()
        wo_free[wb] = last_mm
        o_evac[part][sl] = S.op("scalar",
            lambda e, ppv=ph[sl][:], m=m, part=part: e.activation(
                xc(o_sb, m)[:, part * 512:(part + 1) * 512], ppv, AF.Identity,
                bias=bo_sb[:, m:m + 1], scale=1.0),
            waits=[("pe", last_mm)], inc="act")

    abc_free = [n_A, n_A]
    prev_fin = 0
    prev_fin_pool = 0
    for h in range(H):
        def bmult(j, slot, extra_dve):
            a = j % 2
            bc = 0
            for hf in range(2):
                bc = S.op("tensor", lambda e, a=a, h=h, j=j, hf=hf: e.matmul(
                        ps_abc[a][:, hf * 512:(hf + 1) * 512],
                        sel_bf[:, h * 128:(h + 1) * 128],
                        A_sb[:, j * T + hf * 512:j * T + (hf + 1) * 512],
                        start=True, stop=True, skip_group_check=True),
                     waits=[("dve", abc_free[a]), ("act", n_v_evac),
                            ("wq", n_selw)], inc="pe")
            md = S.op("vector", lambda e, a=a, j=j, slot=slot: e.tensor_tensor(
                    t8[slot][:], xc(v_all, j), ps_abc[a][:], ALU.mult),
                 waits=[("pe", bc), ("dve", extra_dve),
                        ("pool", prev_fin_pool)], inc="dve")
            abc_free[a] = md
            return md
        m0 = bmult(0, 0, prev_fin)
        m1 = bmult(1, 1, prev_fin)
        s01 = S.op("vector", lambda e: e.tensor_tensor(t4[0][:], t8[0][:],
                                                       t8[1][:], ALU.add),
                   waits=[("dve", m1)], inc="dve")
        m2 = bmult(2, 2, prev_fin)
        m3 = bmult(3, 3, prev_fin)
        s23 = S.op("gpsimd", lambda e: e.tensor_tensor(t4[1][:], t8[2][:],
                                                       t8[3][:], ALU.add),
                   waits=[("dve", m3)], inc="pool")
        m4 = bmult(4, 0, s01)
        m5 = bmult(5, 1, s01)
        s03 = S.op("vector", lambda e: e.tensor_tensor(t4[0][:], t4[0][:],
                                                       t4[1][:], ALU.add),
                   waits=[("pool", s23)], inc="dve")
        s45 = S.op("vector", lambda e: e.tensor_tensor(t4[1][:], t8[0][:],
                                                       t8[1][:], ALU.add),
                   waits=[("dve", m5)], inc="dve")
        m6 = bmult(6, 2, s03)
        m7 = bmult(7, 3, s03)
        s67 = S.op("gpsimd", lambda e: e.tensor_tensor(t4[2][:], t8[2][:],
                                                       t8[3][:], ALU.add),
                   waits=[("dve", m7)], inc="pool")
        s47 = S.op("gpsimd", lambda e: e.tensor_tensor(t4[1][:], t4[1][:],
                                                       t4[2][:], ALU.add),
                   waits=[("dve", s45)], inc="pool")
        prev_fin = S.op("vector", lambda e, h=h: e.tensor_tensor(
                xc(attn, h), t4[0][:], t4[1][:], ALU.add),
             waits=[("dve", s03), ("pool", s47)], inc="dve")
        prev_fin_pool = s47
        fin_of[h] = prev_fin
        if h < 8:
            emit_ln2_apply(2 * h)
            emit_ln2_apply(2 * h + 1)
        if h == 7:
            psf("ps_b2O"); psf("ps_b1O")
            ph.append(psa("phA", [128, 512]))
            ph.append(psa("phB", [128, 512]))
        if h == 7:
            for mpre in range(3):
                emit_oprj_dma(mpre)
        if h >= 8:
            for mo in (2 * (h - 8), 2 * (h - 8) + 1):
                emit_oprj(0, mo)
                if mo + 3 < KC:
                    emit_oprj_dma(mo + 3)
    n_attn = prev_fin
    if _UPTO == "ST2":
        return finish(dict(locals()))

    sbf("xOst0")
    corr_sb = sba("corr_sb", [128, T], BF16)
    corrh_sb = sba("corrh_sb", [128, T], BF16)
    ps_S = psa("ps_S", [1, T])
    for c in range(KC):
        for hf in range(2):
            S.op("tensor", MM(ps_S[:, hf * 512:(hf + 1) * 512], ones_c[:],
                              xc(xn_s, c)[:, hf * 512:(hf + 1) * 512],
                              c == 0, c == KC - 1),
                 waits=[("dve", n_apply)], inc="pe")
    n_S = S.n("pe")
    v_s128 = S.op("vector", lambda e: e.tensor_scalar(brow[1][:], ps_S[:],
                                                      128.0, None, ALU.mult),
                  waits=[("pe", n_S)], inc="dve")
    psf("ps_S")
    ps_bcC = psa("ps_bcC", [128, T])
    for hf in range(2):
        S.op("tensor", MM(ps_bcC[:, hf * 512:(hf + 1) * 512], ones_r[:],
                          brow[1][:, hf * 512:(hf + 1) * 512], True, True),
             waits=[("dve", v_s128)], inc="pe")
    n_cb = S.n("pe")
    a_corr = S.op("scalar", lambda e: e.activation(corr_sb[:], ps_bcC[:], AF.Copy),
                  waits=[("pe", n_cb)], inc="act")
    psf("ps_bcC")
    for mpre in range(3):
        emit_oprj_dma(mpre)
    for m in range(KC):
        emit_oprj(1, m)
        if m + 3 < KC:
            emit_oprj_dma(m + 3)
    n_oall = max(o_evac[1])
    psf("phB"); psf("phA")
    psf("ps_abc1"); psf("ps_abc0")
    sbf("t8s"); sbf("sel_bf"); sbf("v_all")
    sbf("attn"); sbf("A_sb"); sbf("sel8_bf"); sbf("ind8_bf")
    sbf("rn8_bf"); sbf("c_a2"); sbf("c_a1"); sbf("sqt2")
    sbf("tmpb3"); sbf("sin_bf"); sbf("cos_bf")
    if _UPTO == "OPRJ":
        return finish(dict(locals()))

    h_sb = sba("h_sb", [128, FC * T], BF16, side="right")
    for nm in ("ppGA", "ppGB", "ppGC"):
        pp.append(psa(nm, [128, T]))
    ps_Sh = psa("ps_Sh", [1, T])
    pp_free = [0, 0, 0]
    w_free = [wo_free[0], wo_free[1], wo_free[2]]
    silu_done = 0
    h_done = 0
    last_mm = 0
    sh_pend = []
    for b in range(2 * FC):
        c, is_up = b // 2, b % 2
        sl = b % 3
        wb = b % 3
        S.dma("gpsimd", wv[wb], d["wgu"][b], waits=[("pe", w_free[wb])])
        n_w = S.n("wq")
        waits = [("wq", n_w), ("dve", pp_free[sl]), ("dve", n_attn),
                 ("act", a_corr), ("act", n_oall)]
        for kc in range(KC):
            for hf in range(2):
                last_mm = S.op("tensor",
                    MM(pp[sl][:, hf * 512:(hf + 1) * 512],
                       wv[wb][:, kc * 128:(kc + 1) * 128],
                       xc(xn_s, kc)[:, hf * 512:(hf + 1) * 512],
                       kc == 0, kc == KC - 1),
                    waits=waits, inc="pe")
                waits = ()
        w_free[wb] = last_mm
        if not is_up:
            g_sub = S.op("vector", lambda e, ppv=pp[sl][:]: e.tensor_tensor(
                    tmpf[0][:], ppv, corr_sb[:], ALU.subtract),
                 waits=[("pe", last_mm), ("act", silu_done)], inc="dve")
            pp_free[sl] = g_sub
            silu_done = S.op("scalar", lambda e, c=c: e.activation(
                    tmpf[1][:], tmpf[0][:], AF.Silu, bias=0.0,
                    scale=sg_sb[:, c:c + 1]),
                 waits=[("dve", g_sub)], inc="act")
        else:
            u_sub = S.op("vector", lambda e, ppv=pp[sl][:]: e.tensor_tensor(
                    tmpf[2][:], ppv, corr_sb[:], ALU.subtract),
                 waits=[("pe", last_mm)], inc="dve")
            pp_free[sl] = u_sub
            h_done = S.op("vector", lambda e, c=c: e.scalar_tensor_tensor(
                    xc(h_sb, c), tmpf[2][:], su_sb[:, c:c + 1], tmpf[1][:],
                    ALU.mult, ALU.mult),
                 waits=[("act", silu_done)], inc="dve")
            sh_pend.append((c, h_done))
        while len(sh_pend) > (1 if b < 2 * FC - 1 else 0):
            cc, hd = sh_pend.pop(0)
            for hf in range(2):
                S.op("tensor", MM(ps_Sh[:, hf * 512:(hf + 1) * 512], ones_c[:],
                                  xc(h_sb, cc)[:, hf * 512:(hf + 1) * 512],
                                  cc == 0, cc == FC - 1),
                     waits=[("dve", hd)], inc="pe")
    n_Sh = S.n("pe")
    n_hall = h_done
    v_sh = S.op("vector", lambda e: e.tensor_scalar(brow[1][:], ps_Sh[:],
                                                    128.0, None, ALU.mult),
                waits=[("pe", n_Sh)], inc="dve")
    for hf in range(2):
        S.op("tensor", MM(pp[0][:, hf * 512:(hf + 1) * 512], ones_r[:],
                          brow[1][:, hf * 512:(hf + 1) * 512], True, True),
             waits=[("dve", v_sh), ("dve", pp_free[0])], inc="pe")
    n_chbc = S.n("pe")
    a_corrh = S.op("scalar", lambda e, ppv=pp[0][:]: e.activation(corrh_sb[:], ppv, AF.Copy),
                   waits=[("pe", n_chbc)], inc="act")
    psf("ps_Sh")
    if _UPTO == "GUP":
        return finish(dict(locals()))

    gtmp = sba("gtmp", [128, T], BF16, side="right")
    xf = [sba(f"xf{i}", [128, T], F32, side="right") for i in range(2)]
    yT_v = yT.rearrange("(c p) t -> c p t", c=KC, p=128)
    xO_v = d["xO"].rearrange("(c p) t -> c p t", c=KC, p=128)
    wd_free = [S.n("pe"), S.n("pe")]
    g_done = 0
    pp_free3 = [S.n("dve")] * 3
    xf_free = [n_hall, n_hall]
    out_free = [0, 0]
    last_r = 0
    for m in range(KC):
        sl = m % 3
        S.dma("gpsimd", wdv[0], d["wd"][m][:, 0:2816],
              waits=[("pe", wd_free[0])])
        n_w0 = S.n("wq")
        S.dma("gpsimd", wdv[1], d["wd"][m][:, 2816:5632],
              waits=[("pe", wd_free[1])])
        n_w1 = S.n("wq")
        S.dma("sync", xf[m % 2][:], xO_v[m], waits=[("dve", xf_free[m % 2])],
              inc="xq")
        n_x = S.n("xq")
        waits = [("wq", n_w0), ("dve", pp_free3[sl]), ("dve", n_hall),
                 ("act", a_corrh)]
        last_mm = 0
        for kc in range(FC):
            if kc == 22:
                wd_free[0] = last_mm
                waits = [("wq", n_w1)]
            wsrc = (wdv[0][:, kc * 128:(kc + 1) * 128] if kc < 22 else
                    wdv[1][:, (kc - 22) * 128:(kc - 21) * 128])
            for hf in range(2):
                last_mm = S.op("tensor",
                    MM(pp[sl][:, hf * 512:(hf + 1) * 512], wsrc,
                       xc(h_sb, kc)[:, hf * 512:(hf + 1) * 512],
                       kc == 0, kc == FC - 1),
                    waits=waits, inc="pe")
                waits = ()
        wd_free[1] = last_mm
        d_sub = S.op("vector", lambda e, ppv=pp[sl][:]: e.tensor_tensor(
                tmpf[0][:], ppv, corrh_sb[:], ALU.subtract),
             waits=[("pe", last_mm), ("act", g_done), ("dve", last_r)],
             inc="dve")
        pp_free3[sl] = d_sub
        g_done = S.op("scalar", lambda e, m=m: e.activation(
                gtmp[:], tmpf[0][:], AF.Silu, bias=0.0,
                scale=sd_sb[:, m:m + 1]),
             waits=[("dve", d_sub)], inc="act")
        og = S.op("vector", lambda e, m=m: e.tensor_tensor(
                tmpf[0][:], gtmp[:], xc(o_sb, m), ALU.mult),
             waits=[("act", g_done), ("act", n_oall)], inc="dve")
        last_r = S.op("vector", lambda e, m=m: e.tensor_tensor(
                tmpf[1 + (m % 2)][:], tmpf[0][:], xf[m % 2][:], ALU.add),
             waits=[("xq", n_x), ("st", out_free[m % 2])], inc="dve")
        xf_free[m % 2] = last_r
        out_free[m % 2] = S.op("sync", lambda e, m=m: e.dma_start(
                yT_v[m], tmpf[1 + (m % 2)][:]),
             waits=[("dve", last_r)], inc="st", amt=16)

    return finish(dict(locals()))


_NC_CACHE = {}


def kernel(**inputs):
    per_core = _prep_host(inputs)
    key = ("prog", _UPTO, tuple(_DEBUG_TAPS))
    if key not in _NC_CACHE:
        _NC_CACHE.clear()
        _NC_CACHE[key] = build_program()
    nc = _NC_CACHE[key]
    res = run_bass_kernel_spmd(nc, per_core, core_ids=list(range(NCORES)))
    outs = [np.asarray(r["yT"], dtype=np.float32) for r in res.results]
    y = np.zeros((4, 2048, DM), dtype=np.float32)
    for c in range(NCORES):
        b, half = c // 2, c % 2
        y[b, _out_cols(half), :] = outs[c].T
    if _DEBUG_TAPS:
        kernel._taps = [{k: np.asarray(v) for k, v in r.items()
                         if k.startswith("tap_")} for r in res.results]
    return y
